# version 01
# baseline (speedup 1.0000x reference)
import contextlib
import numpy as np
import ml_dtypes
import concourse.bass as bass
import concourse.mybir as mybir
from concourse.bass_utils import run_bass_kernel_spmd

F32 = mybir.dt.float32
BF16 = mybir.dt.bfloat16
AF = mybir.ActivationFunctionType
ALU = mybir.AluOpType
AX = mybir.AxisListType
BF = ml_dtypes.bfloat16

B, L, D = 4, 4096, 2048
H, HD, FD = 32, 64, 64
TOK = 2048
NG = 2
GT = TOK // NG
NCH = GT // 128
NHG = 4
HPG = H // NHG
NE = 32
DFF = 768
GU2 = 2 * DFF
KP = D // 128
NTP = TOK // 128
EPS_LN = 1e-5
EPS_AT = 1e-12
RD = 1.0 / D

ENGS = ("sync", "scalar", "vector", "tensor", "gpsimd")


class Sem:
    def __init__(self, handle):
        self.h = handle
        self.n = 0

    def bump(self, k):
        self.n += k
        return self.n


class Plan:

    def __init__(self):
        self.prog = {e: [] for e in ENGS}
        self.auto = {}

    def em(self, eng, fn):
        self.prog[eng].append(fn)

    def wait(self, eng, sem, val):
        if val > 0:
            self.em(eng, lambda E, s=sem.h, v=val: E.wait_ge(s, v))

    def dma(self, eng, out, in_, sem, transpose=False, accum=False):
        def f(E, out=out, in_=in_, t=transpose, a=accum, s=sem.h):
            kw = {}
            if t:
                kw["transpose"] = True
            if a:
                kw["accum_op"] = ALU.add
            E.dma_start(out=out, in_=in_, **kw).then_inc(s, 16)
        self.em(eng, f)
        return sem.bump(16)

    def op(self, eng, name, *args, inc=None, **kw):
        auto = self.auto.get(eng)
        if auto is not None:
            self.wait(eng, auto, auto.n)
            inc = auto
        def f(E, name=name, args=args, kw=kw, inc=inc):
            ins = getattr(E, name)(*args, **kw)
            if inc is not None:
                ins.then_inc(inc.h, 1)
        self.em(eng, f)
        return inc.bump(1) if inc is not None else None

    def mm(self, out, lhsT, rhs, start, stop, inc=None):
        def f(T, out=out, l=lhsT, r=rhs, s=start, t=stop, inc=inc):
            ins = T.matmul(out, l, r, start=s, stop=t)
            if inc is not None:
                ins.then_inc(inc.h, 1)
        self.em("tensor", f)
        return inc.bump(1) if inc is not None else None


def build_program(nc, cfg=None):
    cfg = dict(cfg or {})
    do_attn = cfg.get("attn", True)
    do_moe = cfg.get("moe", True)
    n_exp = cfg.get("n_experts", NE)

    x_main = nc.dram_tensor("x_main", [TOK, D], F32, kind="ExternalInput").ap()
    x_pre = nc.dram_tensor("x_pre", [TOK, D], F32, kind="ExternalInput").ap()
    flag = nc.dram_tensor("flag", [128, 1], F32, kind="ExternalInput").ap()
    wq = nc.dram_tensor("wq", [D, D], F32, kind="ExternalInput").ap()
    wk = nc.dram_tensor("wk", [D, D], F32, kind="ExternalInput").ap()
    wv = nc.dram_tensor("wv", [D, D], F32, kind="ExternalInput").ap()
    wo = nc.dram_tensor("wo", [D, D], F32, kind="ExternalInput").ap()
    wfq = nc.dram_tensor("wfq", [H, HD, FD], F32, kind="ExternalInput").ap()
    wfk = nc.dram_tensor("wfk", [H, HD, FD], F32, kind="ExternalInput").ap()
    gate = nc.dram_tensor("gate", [NE, D], F32, kind="ExternalInput").ap()
    gup = nc.dram_tensor("gup", [NE, GU2, D], F32, kind="ExternalInput").ap()
    gdn = nc.dram_tensor("gdn", [NE, D, DFF], F32, kind="ExternalInput").ap()
    sim = nc.dram_tensor("sim", [NE, NE], F32, kind="ExternalInput").ap()
    maskU_f = nc.dram_tensor("maskU_f", [128, 128], F32, kind="ExternalInput").ap()
    maskU_b = nc.dram_tensor("maskU_b", [128, 128], BF16, kind="ExternalInput").ap()
    ident_f = nc.dram_tensor("ident_f", [128, 128], F32, kind="ExternalInput").ap()
    ones_col_b = nc.dram_tensor("ones_col_b", [128, 1], BF16, kind="ExternalInput").ap()
    ones_row_b = nc.dram_tensor("ones_row_b", [1, 128], BF16, kind="ExternalInput").ap()
    zero_b = nc.dram_tensor("zero_b", [128, 128], BF16, kind="ExternalInput").ap()

    out_hbm = nc.dram_tensor("out", [TOK, D], F32, kind="ExternalOutput").ap()

    wv_bf = nc.dram_tensor("wv_bf", [D, D], BF16, kind="Internal").ap()
    wo_bf = nc.dram_tensor("wo_bf", [D, D], BF16, kind="Internal").ap()
    gate_bf = nc.dram_tensor("gate_bf", [NE, D], BF16, kind="Internal").ap()
    gup_bf = nc.dram_tensor("gup_bf", [NE, GU2, D], BF16, kind="Internal").ap()
    gdn_bf = nc.dram_tensor("gdn_bf", [NE, D, DFF], BF16, kind="Internal").ap()
    ln_scr = nc.dram_tensor("ln_scr", [TOK, D], BF16, kind="Internal").ap()
    ln2_scr = nc.dram_tensor("ln2_scr", [TOK, D], BF16, kind="Internal").ap()
    wqf_scr = nc.dram_tensor("wqf_scr", [NHG, 128, KP * 512], BF16, kind="Internal").ap()
    wkf_scr = nc.dram_tensor("wkf_scr", [NHG, 128, KP * 512], BF16, kind="Internal").ap()

    plan = Plan()
    stack = contextlib.ExitStack()
    E = stack.enter_context

    sLD = Sem(E(nc.semaphore("sLD")))
    sSC = Sem(E(nc.semaphore("sSC")))
    sSW = Sem(E(nc.semaphore("sSW")))
    sPE = Sem(E(nc.semaphore("sPE")))
    sDV = Sem(E(nc.semaphore("sDV")))
    sAC = Sem(E(nc.semaphore("sAC")))
    plan.auto["vector"] = sDV
    plan.auto["scalar"] = sAC

    pbank = [E(nc.psum_tensor(f"pb{i}", [128, 512], F32)) for i in range(8)]
    P_KVA, P_KVB, P_KRA, P_KRB, P_ZK, P_ZQ, P_T1, P_T2 = pbank

    def kv_region(h):
        bank = P_KVA if h < 16 else P_KVB
        hh = h % 16
        return bank[64 * (hh % 2):64 * (hh % 2) + 64,
                    64 * (hh // 2):64 * (hh // 2) + 64]

    def krow_region(h):
        bank = P_KRA if h < 16 else P_KRB
        hh = h % 16
        base = 64 * (hh % 2)
        return bank[base:base + 1, 64 * (hh // 2):64 * (hh // 2) + 64]

    kcum_ps = P_T1[:, 0:64]
    st_ps = P_T1[:, 64:192]
    y_ps = P_T1[0:64, 192:320]
    fqsT_ps = P_T1[0:64, 320:448]
    ekT_ps = P_T2[0:64, 0:128]

    c_maskU_f = E(nc.sbuf_tensor("c_maskU_f", [128, 128], F32))
    c_maskU_b = E(nc.sbuf_tensor("c_maskU_b", [128, 128], BF16))
    c_ident_f = E(nc.sbuf_tensor("c_ident_f", [128, 128], F32))
    c_ones_col = E(nc.sbuf_tensor("c_ones_col", [128, 1], BF16))
    c_ones_row = E(nc.sbuf_tensor("c_ones_row", [1, 128], BF16))
    c_zero = E(nc.sbuf_tensor("c_zero", [128, 128], BF16))
    c_flag = E(nc.sbuf_tensor("c_flag", [128, 1], F32))
    sim_sb = E(nc.sbuf_tensor("sim_sb", [32, 32], BF16))

    ln_stack = contextlib.ExitStack()
    LE = ln_stack.enter_context
    lnx = [LE(nc.sbuf_tensor(f"lnx{i}", [128, D], F32)) for i in range(2)]
    lnsq = LE(nc.sbuf_tensor("lnsq", [128, D], F32))
    lnout = [LE(nc.sbuf_tensor(f"lnout{i}", [128, D], BF16)) for i in range(2)]
    st_nsum = LE(nc.sbuf_tensor("st_nsum", [128, 1], F32))
    st_ssq = LE(nc.sbuf_tensor("st_ssq", [128, 1], F32))
    st_nmean = LE(nc.sbuf_tensor("st_nmean", [128, 1], F32))
    st_var = LE(nc.sbuf_tensor("st_var", [128, 1], F32))
    st_rstd = LE(nc.sbuf_tensor("st_rstd", [128, 1], F32))

    for dst, src in [(c_maskU_f, maskU_f), (c_maskU_b, maskU_b),
                     (c_ident_f, ident_f), (c_ones_col, ones_col_b),
                     (c_ones_row, ones_row_b), (c_zero, zero_b),
                     (c_flag, flag)]:
        plan.dma("sync", dst[:], src, sLD)
    consts_ld = sLD.n
    for eng in ENGS:
        plan.wait(eng, sLD, consts_ld)

    sim_ld = plan.dma("gpsimd", sim_sb[:], sim, sSW)

    lnout_store = [0, 0]
    ln_scr_free_ld = [0]
    state_copied = {h: 0 for h in range(H)}
    kv_pe = {h: 0 for h in range(H)}
    x2_store = [0, 0]

    lnx_dv = [0, 0]

    def ln_phase(src, g, dst):
        for i in range(NCH):
            par = i % 2
            buf = lnx[par]
            lb = lnout[par]
            row = 128 * (NCH * g + i)
            plan.wait("scalar", sDV, lnx_dv[par])
            v = plan.dma("scalar", buf[:], src[row:row + 128, :], sSC)
            plan.wait("vector", sSC, v)
            plan.op("vector", "reduce_sum", st_nsum[:], buf[:], AX.X, negate=True, inc=sDV)
            plan.op("vector", "tensor_mul", lnsq[:], buf[:], buf[:], inc=sDV)
            plan.op("vector", "reduce_sum", st_ssq[:], lnsq[:], AX.X, inc=sDV)
            plan.op("vector", "tensor_scalar_mul", st_nmean[:], st_nsum[:], RD, inc=sDV)
            plan.op("vector", "tensor_scalar_mul", st_ssq[:], st_ssq[:], RD, inc=sDV)
            plan.op("vector", "tensor_mul", st_nsum[:], st_nmean[:], st_nmean[:], inc=sDV)
            plan.op("vector", "tensor_sub", st_var[:], st_ssq[:], st_nsum[:], inc=sDV)
            dvv = plan.op("vector", "tensor_scalar", st_var[:], st_var[:], EPS_LN,
                          None, ALU.add, inc=sDV)
            plan.wait("scalar", sDV, dvv)
            ac = plan.op("scalar", "activation", st_rstd[:], st_var[:], AF.Ln, inc=sAC)
            plan.wait("vector", sAC, ac)
            dvv = plan.op("vector", "tensor_scalar", st_rstd[:], st_rstd[:], -0.5,
                          None, ALU.mult, inc=sDV)
            plan.wait("scalar", sDV, dvv)
            ac = plan.op("scalar", "activation", st_rstd[:], st_rstd[:], AF.Exp, inc=sAC)
            plan.wait("vector", sAC, ac)
            plan.wait("vector", sSW, lnout_store[par])
            dvv = plan.op("vector", "tensor_scalar", lb[:], buf[:],
                          st_nmean[:, 0:1], st_rstd[:, 0:1], ALU.add, ALU.mult, inc=sDV)
            lnx_dv[par] = dvv
            plan.wait("gpsimd", sDV, dvv)
            if i == 0:
                plan.wait("gpsimd", sLD, ln_scr_free_ld[0])
            lnout_store[par] = plan.dma(
                "gpsimd", dst[row:row + 128, :], lb[:], sSW)
        return sSW.n


    if do_attn:
        at = contextlib.ExitStack()
        A = at.enter_context
        lnT_buf = A(nc.sbuf_tensor("lnT_buf", [128, KP * GT], BF16, is_transpose_dest=True))
        v_buf = A(nc.sbuf_tensor("v_buf", [128, NCH * D], BF16))
        yt_buf = A(nc.sbuf_tensor("yt_buf", [128, KP * GT], BF16))
        wt_buf = A(nc.sbuf_tensor("wt_buf", [128, KP * 1024], BF16, is_transpose_dest=True))
        wqkf_buf = wt_buf
        wfq_sb = A(nc.sbuf_tensor("wfq_sb", [64, H * FD], BF16))
        wfk_sb = A(nc.sbuf_tensor("wfk_sb", [64, H * FD], BF16))
        wrow_buf = [A(nc.sbuf_tensor(f"wrow{i}", [64, D], BF16)) for i in range(2)]
        state_sb = A(nc.sbuf_tensor("state_sb", [128, H * HD], BF16))
        krow_sb = A(nc.sbuf_tensor("krow_sb", [1, H * FD], BF16))
        x2h = [A(nc.sbuf_tensor(f"x2h{i}", [128, 1024], F32)) for i in range(2)]
        ek8 = A(nc.sbuf_tensor("ek8", [128, HPG * 64], F32))
        eq8 = A(nc.sbuf_tensor("eq8", [128, HPG * 64], F32))
        fkb8 = A(nc.sbuf_tensor("fkb8", [128, HPG * 64], BF16))
        ekT8 = A(nc.sbuf_tensor("ekT8", [64, HPG * 128], BF16))
        rk8 = A(nc.sbuf_tensor("rk8", [128, HPG], F32))
        nm8 = A(nc.sbuf_tensor("nm8", [128, HPG], F32))
        fqs_sb = A(nc.sbuf_tensor("fqs_sb", [128, 64], F32))
        fqsT_sb = A(nc.sbuf_tensor("fqsT_sb", [128, 128], BF16))
        sm_sb = A(nc.sbuf_tensor("sm_sb", [128, 128], BF16))
        dtmp_sb = A(nc.sbuf_tensor("dtmp_sb", [128, 64], F32))
        den_sb = A(nc.sbuf_tensor("den_sb", [128, 1], F32))
        rden_sb = A(nc.sbuf_tensor("rden_sb", [128, 1], F32))

        plan.op("vector", "memset", state_sb[:], 0.0)
        plan.op("vector", "memset", fqsT_sb[:], 0.0)
        dbg_core = None
        if cfg.get("dbg_core"):
            shapes = {"kcum": (128, 64), "st": (128, 128), "rden": (128, 1),
                      "fqs": (128, 64), "sm": (128, 128), "eq": (128, 64),
                      "ek": (128, 64), "fkb": (128, 64), "ekT": (64, 128),
                      "rk": (128, 1)}
            bufs = {nm: A(nc.sbuf_tensor(f"dbgc_{nm}", list(sh), F32))
                    for nm, sh in shapes.items()}
            dbg_core = {"armed": True, "bufs": bufs, "shapes": shapes}

        plan.dma("gpsimd", wfq_sb[:].rearrange("d (h f) -> d h f", f=FD),
                 wfq.rearrange("h d f -> d h f"), sSW)
        wf_ld = plan.dma("gpsimd", wfk_sb[:].rearrange("d (h f) -> d h f", f=FD),
                         wfk.rearrange("h d f -> d h f"), sSW)
        plan.wait("tensor", sSW, wf_ld)
        wrow_pe = [0, 0]
        for grp in range(NHG):
            plan.wait("vector", sLD, sLD.n)
            for side, (w_in, wf_sb, scr) in enumerate(
                    [(wq, wfq_sb, wqf_scr), (wk, wfk_sb, wkf_scr)]):
                for hh in range(HPG):
                    h = grp * HPG + hh
                    par = hh % 2
                    wb = wrow_buf[par]
                    plan.wait("gpsimd", sPE, wrow_pe[par])
                    ld = plan.dma("gpsimd", wb[:], w_in[64 * h:64 * h + 64, :], sSW)
                    plan.wait("tensor", sSW, ld)
                    pe_half = []
                    for p in range(KP):
                        bank = P_ZK if p < 8 else P_ZQ
                        col = 64 * (p % 8)
                        last = plan.mm(bank[:, col:col + 64],
                                       wb[:, 128 * p:128 * (p + 1)],
                                       wf_sb[:, 64 * h:64 * h + 64],
                                       start=True, stop=True,
                                       inc=(sPE if (p % 8) == 7 else None))
                        if (p % 8) == 7:
                            pe_half.append(last)
                    wrow_pe[par] = pe_half[-1]
                    reg = wqkf_buf[:, (side * KP * 512):((side + 1) * KP * 512)]
                    reg3 = reg.rearrange("a (p f) -> a p f", f=512)
                    dvl = 0
                    for half, bank in enumerate((P_ZK, P_ZQ)):
                        plan.wait("vector", sPE, pe_half[half])
                        src3 = bank[:, 0:512].rearrange("a (p f) -> a p f", f=64)
                        dvl = plan.op("vector", "tensor_copy",
                                      reg3[:, 8 * half:8 * half + 8,
                                           64 * hh:64 * hh + 64],
                                      src3, inc=sDV)
                    plan.wait("tensor", sDV, dvl)
            plan.wait("sync", sDV, sDV.n)
            plan.dma("sync", wqf_scr[grp], wqkf_buf[:, 0:KP * 512], sLD)
            plan.dma("sync", wkf_scr[grp], wqkf_buf[:, KP * 512:2 * KP * 512], sLD)
        wqkf_stored = sLD.n

        plan.dma("gpsimd", wv_bf, wv, sSW)
        wv_cast = sSW.n
        plan.dma("gpsimd", wo_bf, wo, sSW)
        wo_cast = sSW.n
        plan.dma("gpsimd", gate_bf, gate, sSW)
        gate_cast = sSW.n
        exp_cast = []
        for e in range(n_exp):
            plan.dma("gpsimd", gup_bf[e], gup[e], sSW)
            plan.dma("gpsimd", gdn_bf[e], gdn[e], sSW)
            exp_cast.append(sSW.n)

        for bank in (P_KVA, P_KVB, P_KRA, P_KRB):
            for q in range(4):
                plan.mm(bank[:, 128 * q:128 * (q + 1)], c_zero[:], c_maskU_b[:],
                        start=True, stop=True)
        init_pe = plan.mm(P_T2[:, 256:384], c_zero[:], c_maskU_b[:],
                          start=True, stop=True, inc=sPE)

        def lnT_load(g):
            plan.wait("sync", sSW, lnout_store[0])
            plan.wait("sync", sSW, lnout_store[1])
            plan.wait("sync", sPE, sPE.n)
            for p in range(KP):
                plan.dma("sync", lnT_buf[:, GT * p:GT * (p + 1)],
                         ln_scr[GT * g:GT * (g + 1), 128 * p:128 * (p + 1)],
                         sLD, transpose=True)
            ln_scr_free_ld[0] = sLD.n
            return sLD.n

        def v_proj(wait_lnT):
            for j in range(2):
                plan.wait("sync", sSW, wv_cast)
                plan.wait("sync", sPE, sPE.n)
                for p in range(KP):
                    plan.dma("sync", wt_buf[:, 1024 * p:1024 * (p + 1)],
                             wv_bf[1024 * j:1024 * (j + 1), 128 * p:128 * (p + 1)],
                             sLD, transpose=True)
                wtl = sLD.n
                plan.wait("tensor", sLD, max(wtl, wait_lnT))
                plan.wait("tensor", sDV, sDV.n)
                plan.wait("tensor", sAC, sAC.n)
                for tc in range(NCH):
                    for j2 in range(2):
                        bank = pbank[4 + ((tc * 2 + j2) % 3)]
                        pe_l = None
                        for p in range(KP):
                            pe_l = plan.mm(
                                bank[:, 0:512],
                                lnT_buf[:, GT * p + 128 * tc:GT * p + 128 * (tc + 1)],
                                wt_buf[:, 1024 * p + 512 * j2:1024 * p + 512 * (j2 + 1)],
                                start=(p == 0), stop=(p == KP - 1),
                                inc=(sPE if p == KP - 1 else None))
                        plan.wait("vector", sPE, pe_l)
                        dv = plan.op("vector", "tensor_copy",
                                     v_buf[:, D * tc + 1024 * j + 512 * j2:
                                           D * tc + 1024 * j + 512 * (j2 + 1)],
                                     bank[:, 0:512], inc=sDV)
                        plan.wait("tensor", sDV, dv)

        def head_chunk_kside(hg, c, prefix):
            plan.wait("tensor", sDV, sDV.n)
            plan.wait("tensor", sAC, sAC.n)
            pe = None
            zoff = KP * 512
            for p in range(KP):
                pe = plan.mm(P_ZK[:, 0:512],
                             lnT_buf[:, GT * p + 128 * c:GT * p + 128 * (c + 1)],
                             wqkf_buf[:, zoff + 512 * p:zoff + 512 * (p + 1)],
                             start=(p == 0), stop=(p == KP - 1),
                             inc=(sPE if p == KP - 1 else None))
            plan.wait("vector", sPE, pe)
            plan.wait("scalar", sPE, pe)
            for hh in range(HPG):
                h = hg * HPG + hh
                zsl = P_ZK[:, 64 * hh:64 * hh + 64]
                plan.op("vector", "reduce_max", nm8[:, hh:hh + 1], zsl, AX.X,
                        negate=True, inc=sDV)
                plan.wait("scalar", sDV, sDV.n)
                plan.op("scalar", "activation", ek8[:, 64 * hh:64 * hh + 64], zsl,
                        AF.Exp, bias=nm8[:, hh:hh + 1], inc=sAC)
                plan.wait("vector", sAC, sAC.n)
                plan.op("vector", "reduce_sum", den_sb[:], ek8[:, 64 * hh:64 * hh + 64],
                        AX.X, inc=sDV)
                plan.op("vector", "reciprocal", rk8[:, hh:hh + 1], den_sb[:], inc=sDV)
                if prefix:
                    plan.op("vector", "tensor_mul", rk8[:, hh:hh + 1],
                            rk8[:, hh:hh + 1], c_flag[:], inc=sDV)
                plan.wait("scalar", sDV, sDV.n)
                ac = plan.op("scalar", "activation", fkb8[:, 64 * hh:64 * hh + 64],
                             ek8[:, 64 * hh:64 * hh + 64], AF.Copy,
                             scale=rk8[:, hh:hh + 1], inc=sAC)
                plan.wait("tensor", sAC, ac)
                vsl = v_buf[:, D * c + 64 * h:D * c + 64 * h + 64]
                plan.mm(kv_region(h), fkb8[:, 64 * hh:64 * hh + 64], vsl,
                        start=False, stop=True)
                kv_pe[h] = plan.mm(krow_region(h), c_ones_col[:],
                                   fkb8[:, 64 * hh:64 * hh + 64],
                                   start=False, stop=True, inc=sPE)
                if not prefix:
                    pe3 = plan.op("tensor", "transpose", ekT_ps,
                                  ek8[:, 64 * hh:64 * hh + 64], c_ident_f[:], inc=sPE)
                    plan.wait("vector", sPE, pe3)
                    dv = plan.op("vector", "tensor_copy",
                                 ekT8[:, 128 * hh:128 * (hh + 1)], ekT_ps, inc=sDV)
                    plan.wait("tensor", sDV, dv)

        def head_chunk_qside(hg, c):
            plan.wait("tensor", sDV, sDV.n)
            plan.wait("tensor", sAC, sAC.n)
            pe = None
            for p in range(KP):
                pe = plan.mm(P_ZQ[:, 0:512],
                             lnT_buf[:, GT * p + 128 * c:GT * p + 128 * (c + 1)],
                             wqkf_buf[:, 512 * p:512 * (p + 1)],
                             start=(p == 0), stop=(p == KP - 1),
                             inc=(sPE if p == KP - 1 else None))
            plan.wait("vector", sPE, pe)
            plan.wait("scalar", sPE, pe)
            for hh in range(HPG):
                h = hg * HPG + hh
                zsl = P_ZQ[:, 64 * hh:64 * hh + 64]
                plan.op("vector", "reduce_max", nm8[:, hh:hh + 1], zsl, AX.X,
                        negate=True, inc=sDV)
                plan.wait("scalar", sDV, sDV.n)
                ac_eq = plan.op("scalar", "activation", eq8[:, 64 * hh:64 * hh + 64],
                                zsl, AF.Exp, bias=nm8[:, hh:hh + 1], inc=sAC)
                plan.wait("tensor", sDV, state_copied[h])
                plan.mm(kcum_ps, c_maskU_b[:], fkb8[:, 64 * hh:64 * hh + 64],
                        start=True, stop=False)
                pe1 = plan.mm(kcum_ps, c_ones_row[:],
                              krow_sb[0:1, 64 * h:64 * h + 64],
                              start=False, stop=True, inc=sPE)
                plan.wait("vector", sPE, pe1)
                plan.wait("vector", sAC, ac_eq)
                plan.op("vector", "scalar_tensor_tensor", dtmp_sb[:], kcum_ps,
                        EPS_AT, eq8[:, 64 * hh:64 * hh + 64], ALU.add, ALU.mult,
                        inc=sDV)
                plan.op("vector", "reduce_sum", den_sb[:], dtmp_sb[:], AX.X, inc=sDV)
                dv = plan.op("vector", "reciprocal", rden_sb[:], den_sb[:], inc=sDV)
                plan.wait("scalar", sDV, dv)
                ac = plan.op("scalar", "activation", fqs_sb[:],
                             eq8[:, 64 * hh:64 * hh + 64], AF.Copy,
                             scale=rden_sb[:, 0:1], inc=sAC)
                plan.wait("tensor", sAC, ac)
                pe2 = plan.op("tensor", "transpose", fqsT_ps, fqs_sb[:],
                              c_ident_f[:], inc=sPE)
                plan.wait("vector", sPE, pe2)
                dv = plan.op("vector", "tensor_copy", fqsT_sb[0:64, :], fqsT_ps, inc=sDV)
                plan.wait("tensor", sDV, dv)
                pe3 = plan.mm(st_ps, ekT8[:, 128 * hh:128 * (hh + 1)],
                              fqsT_sb[0:64, :], start=True, stop=True, inc=sPE)
                plan.wait("vector", sPE, pe3)
                dv = plan.op("vector", "scalar_tensor_tensor", sm_sb[:], st_ps,
                             rk8[:, hh:hh + 1], c_maskU_f[:], ALU.mult, ALU.mult,
                             inc=sDV)
                plan.wait("tensor", sDV, dv)
                vsl = v_buf[:, D * c + 64 * h:D * c + 64 * h + 64]
                plan.mm(y_ps, vsl, sm_sb[:], start=True, stop=False)
                pe4 = plan.mm(y_ps, state_sb[:, 64 * h:64 * h + 64], fqsT_sb[:],
                              start=False, stop=True, inc=sPE)
                plan.wait("scalar", sPE, pe4)
                ac = plan.op("scalar", "copy",
                             yt_buf[64 * (h % 2):64 * (h % 2) + 64,
                                    GT * (h // 2) + 128 * c:
                                    GT * (h // 2) + 128 * (c + 1)],
                             y_ps, inc=sAC)
                plan.wait("tensor", sAC, ac)
                plan.wait("vector", sPE, kv_pe[h])
                plan.op("vector", "tensor_copy", state_sb[0:64, 64 * h:64 * h + 64],
                        kv_region(h), inc=sDV)
                dv = plan.op("vector", "tensor_copy",
                             krow_sb[0:1, 64 * h:64 * h + 64], krow_region(h),
                             inc=sDV)
                state_copied[h] = dv
                if dbg_core is not None and hg == 0 and hh == 0 and dbg_core.get("armed"):
                    dbg_core["armed"] = False
                    bufs = dbg_core["bufs"]
                    plan.op("vector", "tensor_copy", bufs["kcum"][:], dtmp_sb[:], inc=sDV)
                    plan.op("vector", "tensor_copy", bufs["st"][:], sm_sb[:], inc=sDV)
                    plan.op("vector", "tensor_copy", bufs["rden"][:], rden_sb[:], inc=sDV)
                    plan.op("vector", "tensor_copy", bufs["fqs"][:], fqs_sb[:], inc=sDV)
                    plan.op("vector", "tensor_copy", bufs["sm"][:], sm_sb[:], inc=sDV)
                    plan.op("vector", "tensor_copy", bufs["eq"][:], eq8[:, 0:64], inc=sDV)
                    plan.op("vector", "tensor_copy", bufs["ek"][:], ek8[:, 0:64], inc=sDV)
                    plan.op("vector", "tensor_copy", bufs["fkb"][:], fkb8[:, 0:64], inc=sDV)
                    plan.op("vector", "tensor_copy", bufs["ekT"][:], ekT8[:, 0:128], inc=sDV)
                    dv2 = plan.op("vector", "tensor_copy", bufs["rk"][:], rk8[:, 0:1], inc=sDV)
                    plan.wait("tensor", sDV, dv2)

        for g in range(NG):
            ln_phase(x_pre, g, ln_scr)
            lnT_ld = lnT_load(g)
            v_proj(lnT_ld)
            for hg in range(NHG):
                plan.wait("sync", sLD, wqkf_stored)
                plan.wait("sync", sPE, sPE.n)
                plan.dma("sync", wqkf_buf[:, KP * 512:2 * KP * 512], wkf_scr[hg], sLD)
                plan.wait("tensor", sLD, sLD.n)
                if g == 0 and hg == 0:
                    plan.wait("tensor", sPE, init_pe)
                for c in range(NCH):
                    head_chunk_kside(hg, c, prefix=True)
        plan.wait("vector", sPE, sPE.n)
        for h in range(H):
            plan.op("vector", "tensor_copy", state_sb[0:64, 64 * h:64 * h + 64],
                    kv_region(h), inc=sDV)
            dv = plan.op("vector", "tensor_copy",
                         krow_sb[0:1, 64 * h:64 * h + 64], krow_region(h), inc=sDV)
            state_copied[h] = dv
        if cfg.get("dbg_attn"):
            dt_ = nc.dram_tensor("dbg_state0", [64, H * HD], BF16,
                                 kind="ExternalOutput").ap()
            plan.wait("gpsimd", sDV, sDV.n)
            plan.dma("gpsimd", dt_, state_sb[:], sSW)
            dt_ = nc.dram_tensor("dbg_krow0", [1, H * FD], BF16,
                                 kind="ExternalOutput").ap()
            plan.dma("gpsimd", dt_, krow_sb[:], sSW)

        for g in range(NG):
            ln_phase(x_main, g, ln_scr)
            lnT_ld = lnT_load(g)
            v_proj(lnT_ld)
            for hg in range(NHG):
                plan.wait("sync", sPE, sPE.n)
                plan.dma("sync", wqkf_buf[:, 0:KP * 512], wqf_scr[hg], sLD)
                plan.dma("sync", wqkf_buf[:, KP * 512:2 * KP * 512], wkf_scr[hg], sLD)
                plan.wait("tensor", sLD, sLD.n)
                for c in range(NCH):
                    head_chunk_kside(hg, c, prefix=False)
                    head_chunk_qside(hg, c)
            if cfg.get("dbg_attn") and g == 0:
                dt_ = nc.dram_tensor("dbg_yt0", [128, KP * GT], BF16,
                                     kind="ExternalOutput").ap()
                plan.wait("gpsimd", sAC, sAC.n)
                plan.wait("gpsimd", sDV, sDV.n)
                plan.dma("gpsimd", dt_, yt_buf[:], sSW)
            for j in range(2):
                plan.wait("sync", sSW, wo_cast)
                plan.wait("sync", sPE, sPE.n)
                for p in range(KP):
                    plan.dma("sync", wt_buf[:, 1024 * p:1024 * (p + 1)],
                             wo_bf[1024 * j:1024 * (j + 1), 128 * p:128 * (p + 1)],
                             sLD, transpose=True)
                wtl = sLD.n
                plan.wait("tensor", sLD, wtl)
                plan.wait("tensor", sDV, sDV.n)
                plan.wait("tensor", sAC, sAC.n)
                plan.wait("scalar", sDV, sDV.n)
                for tc in range(NCH):
                    par = tc % 2
                    row = 128 * (NCH * g + tc)
                    plan.wait("scalar", sDV, lnx_dv[par])
                    v = plan.dma("scalar", lnx[par][:, 0:1024],
                                 x_main[row:row + 128, 1024 * j:1024 * (j + 1)], sSC)
                    for dt2 in range(2):
                        bank = pbank[4 + ((tc * 2 + dt2) % 2)]
                        pe_l = None
                        for p in range(KP):
                            pe_l = plan.mm(
                                bank[:, 0:512],
                                yt_buf[:, GT * p + 128 * tc:GT * p + 128 * (tc + 1)],
                                wt_buf[:, 1024 * p + 512 * dt2:
                                       1024 * p + 512 * (dt2 + 1)],
                                start=(p == 0), stop=(p == KP - 1),
                                inc=(sPE if p == KP - 1 else None))
                        plan.wait("vector", sPE, pe_l)
                        if dt2 == 0:
                            plan.wait("vector", sSC, v)
                            plan.wait("vector", sSW, x2_store[par])
                        dv = plan.op("vector", "tensor_add",
                                     x2h[par][:, 512 * dt2:512 * (dt2 + 1)],
                                     bank[:, 0:512],
                                     lnx[par][:, 512 * dt2:512 * (dt2 + 1)], inc=sDV)
                        lnx_dv[par] = dv
                        plan.wait("tensor", sDV, dv)
                    plan.wait("gpsimd", sDV, sDV.n)
                    x2_store[par] = plan.dma(
                        "gpsimd", out_hbm[row:row + 128, 1024 * j:1024 * (j + 1)],
                        x2h[par][:], sSW)
        x2_all_stored = sSW.n
        if cfg.get("dbg_wscr"):
            for nm, t in [("dbg_wqf0", wqf_scr), ("dbg_wkf0", wkf_scr)]:
                dt_ = nc.dram_tensor(nm, [128, KP * 512], BF16,
                                     kind="ExternalOutput").ap()
                plan.dma("gpsimd", dt_, t[0], sSW)
            dt_ = nc.dram_tensor("dbg_wqf1", [128, KP * 512], BF16,
                                 kind="ExternalOutput").ap()
            plan.dma("gpsimd", dt_, wqf_scr[1], sSW)
            dt_ = nc.dram_tensor("dbg_wfq", [64, H * FD], BF16,
                                 kind="ExternalOutput").ap()
            plan.dma("gpsimd", dt_, wfq_sb[:], sSW)
        if dbg_core is not None:
            for nm, sh in dbg_core["shapes"].items():
                dt_ = nc.dram_tensor(f"dbgo_{nm}", list(sh), F32,
                                     kind="ExternalOutput").ap()
                plan.wait("gpsimd", sDV, sDV.n)
                plan.dma("gpsimd", dt_, dbg_core["bufs"][nm][:], sSW)
        if cfg.get("dbg_attn"):
            for nm, t, w in [("dbg_yt", yt_buf, KP * GT), ("dbg_v", v_buf, NCH * D),
                             ("dbg_wqkf", wqkf_buf, 2 * KP * 512)]:
                dt_ = nc.dram_tensor(nm, [128, w], BF16, kind="ExternalOutput").ap()
                plan.wait("gpsimd", sPE, sPE.n)
                plan.wait("gpsimd", sAC, sAC.n)
                plan.dma("gpsimd", dt_, t[:], sSW)
            for nm, t, w in [("dbg_state", state_sb, H * HD)]:
                dt_ = nc.dram_tensor(nm, [64, w], BF16, kind="ExternalOutput").ap()
                plan.wait("gpsimd", sDV, sDV.n)
                plan.dma("gpsimd", dt_, t[:], sSW)
            dt_ = nc.dram_tensor("dbg_krow", [1, H * FD], BF16, kind="ExternalOutput").ap()
            plan.dma("gpsimd", dt_, krow_sb[:], sSW)
        plan.wait("scalar", sSW, x2_all_stored)
        for g in range(NG):
            ln_phase(out_hbm, g, ln2_scr)
        ln2_stored = sSW.n
        at.close()
    else:
        plan.dma("gpsimd", out_hbm, x_main, sSW)
        x2_all_stored = sSW.n
        plan.dma("gpsimd", gate_bf, gate, sSW)
        gate_cast = sSW.n
        exp_cast = []
        for e in range(n_exp):
            plan.dma("gpsimd", gup_bf[e], gup[e], sSW)
            plan.dma("gpsimd", gdn_bf[e], gdn[e], sSW)
            exp_cast.append(sSW.n)
        plan.wait("scalar", sSW, x2_all_stored)
        for g in range(NG):
            ln_phase(x_main, g, ln2_scr)
        ln2_stored = sSW.n

    ln_stack.close()

    if do_moe:
        mo = contextlib.ExitStack()
        M = mo.enter_context
        ln2T_buf = M(nc.sbuf_tensor("ln2T_buf", [128, KP * TOK], BF16,
                                    is_transpose_dest=True))
        wgu_buf = M(nc.sbuf_tensor("wgu_buf", [128, KP * GU2], BF16,
                                   is_transpose_dest=True))
        wd_buf = M(nc.sbuf_tensor("wd_buf", [128, 6 * D], BF16,
                                  is_transpose_dest=True))
        h_buf = M(nc.sbuf_tensor("h_buf", [128, 6 * TOK], BF16))
        gateT_sb = M(nc.sbuf_tensor("gateT_sb", [128, KP * 32], BF16,
                                    is_transpose_dest=True))
        gs_st = [M(nc.sbuf_tensor(f"gs_st{i}", [128, 512], BF16)) for i in range(2)]
        us_st = [M(nc.sbuf_tensor(f"us_st{i}", [128, 512], BF16)) for i in range(2)]
        y_st = [M(nc.sbuf_tensor(f"y_st{i}", [128, D], F32)) for i in range(2)]
        cmb_sb = M(nc.sbuf_tensor("cmb_sb", [128, NTP * 32], F32))
        prob_sb = M(nc.sbuf_tensor("prob_sb", [128, 32], F32))
        mk_sb = M(nc.sbuf_tensor("mk_sb", [128, 8 * 32], F32))
        mkT_sb = M(nc.sbuf_tensor("mkT_sb", [32, 128], BF16))
        rw_sb = M(nc.sbuf_tensor("rw_sb", [128, 8], F32))
        rwn_sb = M(nc.sbuf_tensor("rwn_sb", [128, 8], F32))
        rtmp_sb = M(nc.sbuf_tensor("rtmp_sb", [128, 32], F32))
        r1_sb = M(nc.sbuf_tensor("r1_sb", [128, 1], F32))
        cj_sb = M(nc.sbuf_tensor("cj_sb", [128, 4], F32))
        best_sb = M(nc.sbuf_tensor("best_sb", [128, 1], F32))
        gj_sb = M(nc.sbuf_tensor("gj_sb", [128, 1], F32))
        eqx_sb = M(nc.sbuf_tensor("eqx_sb", [128, 4], F32))
        coefP_sb = M(nc.sbuf_tensor("coefP_sb", [128, 4], F32))
        coefS_sb = M(nc.sbuf_tensor("coefS_sb", [128, 4], F32))

        for eng in ENGS:
            plan.wait(eng, sDV, sDV.n)
            plan.wait(eng, sAC, sAC.n)
            plan.wait(eng, sPE, sPE.n)
            plan.wait(eng, sSW, ln2_stored)

        for p in range(KP):
            plan.dma("sync", ln2T_buf[:, TOK * p:TOK * (p + 1)],
                     ln2_scr[:, 128 * p:128 * (p + 1)], sLD, transpose=True)
        ln2T_ld = sLD.n
        plan.wait("sync", sSW, gate_cast)
        for p in range(KP):
            plan.dma("sync", gateT_sb[:, 32 * p:32 * (p + 1)],
                     gate_bf[:, 128 * p:128 * (p + 1)], sLD, transpose=True)
        gateT_ld = sLD.n

        plan.wait("tensor", sLD, gateT_ld)
        plan.wait("tensor", sSW, sim_ld)
        for tp in range(NTP):
            bank = pbank[6 + (tp % 2)]
            pe = None
            for p in range(KP):
                pe = plan.mm(bank[:, 0:32],
                             ln2T_buf[:, TOK * p + 128 * tp:TOK * p + 128 * (tp + 1)],
                             gateT_sb[:, 32 * p:32 * (p + 1)],
                             start=(p == 0), stop=(p == KP - 1),
                             inc=(sPE if p == KP - 1 else None))
            plan.wait("vector", sPE, pe)
            plan.wait("scalar", sPE, pe)
            plan.op("vector", "reduce_max", r1_sb[:], bank[:, 0:32], AX.X,
                    negate=True, inc=sDV)
            plan.wait("scalar", sDV, sDV.n)
            ac = plan.op("scalar", "activation", prob_sb[:], bank[:, 0:32], AF.Exp,
                         bias=r1_sb[:, 0:1], inc=sAC)
            plan.wait("vector", sAC, ac)
            for k in range(8):
                plan.op("vector", "reduce_max", rw_sb[:, k:k + 1], prob_sb[:],
                        AX.X, inc=sDV)
                plan.op("vector", "tensor_scalar", mk_sb[:, 32 * k:32 * (k + 1)],
                        prob_sb[:], rw_sb[:, k:k + 1], None, ALU.is_equal, inc=sDV)
                plan.op("vector", "scalar_tensor_tensor", prob_sb[:],
                        mk_sb[:, 32 * k:32 * (k + 1)], -2.0, prob_sb[:],
                        ALU.mult, ALU.add, inc=sDV)
            plan.op("vector", "reduce_sum", r1_sb[:], rw_sb[:], AX.X, inc=sDV)
            plan.op("vector", "reciprocal", r1_sb[:], r1_sb[:], inc=sDV)
            plan.op("vector", "tensor_scalar", rwn_sb[:], rw_sb[:], r1_sb[:, 0:1],
                    None, ALU.mult, inc=sDV)
            plan.op("vector", "tensor_copy", coefP_sb[:], rwn_sb[:, 0:4], inc=sDV)
            for jj in range(4):
                k = 4 + jj
                plan.wait("tensor", sDV, sDV.n)
                plan.op("tensor", "transpose", bank[0:32, 64:192],
                        mk_sb[:, 32 * k:32 * (k + 1)], c_ident_f[:], inc=sPE)
                plan.wait("vector", sPE, sPE.n)
                dv = plan.op("vector", "tensor_copy", mkT_sb[:],
                             bank[0:32, 64:192], inc=sDV)
                plan.wait("tensor", sDV, dv)
                pe = plan.mm(bank[:, 192:224], mkT_sb[:], sim_sb[:],
                             start=True, stop=True, inc=sPE)
                plan.wait("vector", sPE, pe)
                for i in range(4):
                    plan.op("vector", "tensor_mul", rtmp_sb[:], bank[:, 192:224],
                            mk_sb[:, 32 * i:32 * (i + 1)], inc=sDV)
                    plan.op("vector", "reduce_sum", cj_sb[:, i:i + 1], rtmp_sb[:],
                            AX.X, inc=sDV)
                dv = plan.op("vector", "reduce_max", best_sb[:], cj_sb[:], AX.X,
                             inc=sDV)
                plan.wait("tensor", sDV, dv)
                plan.op("vector", "tensor_scalar", gj_sb[:], best_sb[:], 0.5, None,
                        ALU.is_ge, inc=sDV)
                plan.op("vector", "tensor_scalar", eqx_sb[:], cj_sb[:],
                        best_sb[:, 0:1], None, ALU.is_equal, inc=sDV)
                for i in range(1, 4):
                    for i2 in range(i):
                        plan.op("vector", "scalar_tensor_tensor", rtmp_sb[:, 0:1],
                                eqx_sb[:, i2:i2 + 1], -1.0, eqx_sb[:, i:i + 1],
                                ALU.mult, ALU.mult, inc=sDV)
                        plan.op("vector", "tensor_add", eqx_sb[:, i:i + 1],
                                eqx_sb[:, i:i + 1], rtmp_sb[:, 0:1], inc=sDV)
                plan.op("vector", "tensor_mul", rtmp_sb[:, 0:1], gj_sb[:],
                        rwn_sb[:, k:k + 1], inc=sDV)
                for i in range(4):
                    plan.op("vector", "tensor_mul", rtmp_sb[:, 1:2],
                            rtmp_sb[:, 0:1], eqx_sb[:, i:i + 1], inc=sDV)
                    plan.op("vector", "tensor_add", coefP_sb[:, i:i + 1],
                            coefP_sb[:, i:i + 1], rtmp_sb[:, 1:2], inc=sDV)
                plan.op("vector", "tensor_sub", coefS_sb[:, jj:jj + 1],
                        rwn_sb[:, k:k + 1], rtmp_sb[:, 0:1], inc=sDV)
            cmb = cmb_sb[:, 32 * tp:32 * (tp + 1)]
            plan.op("vector", "tensor_scalar", cmb, mk_sb[:, 0:32],
                    coefP_sb[:, 0:1], None, ALU.mult, inc=sDV)
            for i in range(1, 4):
                plan.op("vector", "scalar_tensor_tensor", cmb,
                        mk_sb[:, 32 * i:32 * (i + 1)], coefP_sb[:, i:i + 1], cmb,
                        ALU.mult, ALU.add, inc=sDV)
            for jj in range(4):
                k = 4 + jj
                plan.op("vector", "scalar_tensor_tensor", cmb,
                        mk_sb[:, 32 * k:32 * (k + 1)], coefS_sb[:, jj:jj + 1], cmb,
                        ALU.mult, ALU.add, inc=sDV)
        router_dv = sDV.n
        if cfg.get("dbg_router"):
            for nm, t, w in [("dbg_rw", rw_sb, 8), ("dbg_rwn", rwn_sb, 8),
                             ("dbg_mk", mk_sb, 256), ("dbg_prob", prob_sb, 32),
                             ("dbg_cj", cj_sb, 4), ("dbg_coefP", coefP_sb, 4),
                             ("dbg_coefS", coefS_sb, 4)]:
                dt = nc.dram_tensor(nm, [128, w], F32, kind="ExternalOutput").ap()
                plan.wait("gpsimd", sDV, router_dv)
                plan.dma("gpsimd", dt, t[:], sSW)
        if cfg.get("dbg_cmb"):
            dbg_cmb = nc.dram_tensor("dbg_cmb", [128, NTP * 32], F32,
                                     kind="ExternalOutput").ap()
            plan.wait("gpsimd", sDV, router_dv)
            plan.dma("gpsimd", dbg_cmb, cmb_sb[:], sSW)

        plan.wait("tensor", sLD, ln2T_ld)
        gu_free_pe = 0
        wd_free_pe = 0
        h_free_pe = 0
        yacc_prev = x2_all_stored
        gs_dv = [0, 0]
        y_sw = [0, 0]
        bank_free = {i: (None, 0) for i in range(8)}
        for e in range(n_exp):
            plan.wait("sync", sSW, exp_cast[e])
            plan.wait("sync", sPE, gu_free_pe)
            for p in range(KP):
                plan.dma("sync", wgu_buf[:, GU2 * p:GU2 * (p + 1)],
                         gup_bf[e][:, 128 * p:128 * (p + 1)], sLD, transpose=True)
            wgu_ld = sLD.n
            plan.wait("sync", sPE, wd_free_pe)
            for kf in range(6):
                plan.dma("sync", wd_buf[:, D * kf:D * (kf + 1)],
                         gdn_bf[e][:, 128 * kf:128 * (kf + 1)], sLD, transpose=True)
            wd_ld = sLD.n

            plan.wait("tensor", sLD, wgu_ld)
            plan.wait("tensor", sPE, h_free_pe)
            plan.wait("vector", sPE, h_free_pe)
            it = 0
            for fp in range(6):
                for tt in range(4):
                    bA = (2 * it) % 4
                    bB = bA + 1
                    it += 1
                    for bi in (bA, bB):
                        sem, val = bank_free[bi]
                        if sem is not None:
                            plan.wait("tensor", sem, val)
                    peA = peB = None
                    for p in range(KP):
                        peA = plan.mm(
                            pbank[bA][:, 0:512],
                            wgu_buf[:, GU2 * p + 128 * fp:GU2 * p + 128 * (fp + 1)],
                            ln2T_buf[:, TOK * p + 512 * tt:TOK * p + 512 * (tt + 1)],
                            start=(p == 0), stop=(p == KP - 1),
                            inc=(sPE if p == KP - 1 else None))
                    for p in range(KP):
                        peB = plan.mm(
                            pbank[bB][:, 0:512],
                            wgu_buf[:, GU2 * p + 128 * (fp + 6):GU2 * p + 128 * (fp + 7)],
                            ln2T_buf[:, TOK * p + 512 * tt:TOK * p + 512 * (tt + 1)],
                            start=(p == 0), stop=(p == KP - 1),
                            inc=(sPE if p == KP - 1 else None))
                    par = (fp * 4 + tt) % 2
                    plan.wait("scalar", sPE, peA)
                    plan.wait("scalar", sDV, gs_dv[par])
                    acg = plan.op("scalar", "activation", gs_st[par][:],
                                  pbank[bA][:, 0:512], AF.Silu, inc=sAC)
                    plan.wait("vector", sPE, peB)
                    plan.op("vector", "tensor_copy", us_st[par][:],
                            pbank[bB][:, 0:512], inc=sDV)
                    plan.wait("vector", sAC, acg)
                    dvh = plan.op("vector", "tensor_mul",
                                  h_buf[:, TOK * fp + 512 * tt:TOK * fp + 512 * (tt + 1)],
                                  gs_st[par][:], us_st[par][:], inc=sDV)
                    gs_dv[par] = dvh
                    bank_free[bA] = (sAC, sAC.n)
                    bank_free[bB] = (sDV, dvh)
            gu_free_pe = sPE.n
            h_ready_dv = sDV.n

            plan.wait("tensor", sLD, wd_ld)
            plan.wait("tensor", sDV, h_ready_dv)
            for tp in range(NTP):
                par = tp % 2
                for dt in range(4):
                    bi = 4 + (dt % 2)
                    sem, val = bank_free[bi]
                    if sem is not None:
                        plan.wait("tensor", sem, val)
                    pe_l = None
                    for kf in range(6):
                        pe_l = plan.mm(
                            pbank[bi][:, 0:512],
                            h_buf[:, TOK * kf + 128 * tp:TOK * kf + 128 * (tp + 1)],
                            wd_buf[:, D * kf + 512 * dt:D * kf + 512 * (dt + 1)],
                            start=(kf == 0), stop=(kf == 5),
                            inc=(sPE if kf == 5 else None))
                    plan.wait("scalar", sPE, pe_l)
                    if dt == 0:
                        plan.wait("scalar", sDV, router_dv)
                        plan.wait("scalar", sSW, y_sw[par])
                    ac = plan.op("scalar", "activation",
                                 y_st[par][:, 512 * dt:512 * (dt + 1)],
                                 pbank[bi][:, 0:512], AF.Copy,
                                 scale=cmb_sb[:, 32 * tp + e:32 * tp + e + 1],
                                 inc=sAC)
                    bank_free[bi] = (sAC, ac)
                plan.wait("gpsimd", sAC, sAC.n)
                plan.wait("gpsimd", sSW, yacc_prev)
                y_sw[par] = plan.dma("gpsimd", out_hbm[128 * tp:128 * (tp + 1), :],
                                     y_st[par][:], sSW, accum=True)
            yacc_prev = sSW.n
            wd_free_pe = sPE.n
            h_free_pe = sPE.n
        plan.wait("gpsimd", sSW, sSW.n)
        mo.close()
    else:
        plan.wait("gpsimd", sSW, sSW.n)

    with nc.Block() as block:
        @block.sync
        def _(e):
            for f in plan.prog["sync"]:
                f(e)

        @block.scalar
        def _(e):
            for f in plan.prog["scalar"]:
                f(e)

        @block.vector
        def _(e):
            for f in plan.prog["vector"]:
                f(e)

        @block.tensor
        def _(e):
            for f in plan.prog["tensor"]:
                f(e)

        @block.gpsimd
        def _(e):
            for f in plan.prog["gpsimd"]:
                f(e)

    stack.close()
    return nc


def make_consts():
    maskU = np.triu(np.ones((128, 128), dtype=np.float32))
    return {
        "maskU_f": maskU,
        "maskU_b": maskU.astype(BF),
        "ident_f": np.eye(128, dtype=np.float32),
        "ones_col_b": np.ones((128, 1), dtype=BF),
        "ones_row_b": np.ones((1, 128), dtype=BF),
        "zero_b": np.zeros((128, 128), dtype=BF),
    }


def make_in_maps(x, shared):
    consts = make_consts()
    in_maps = []
    for c in range(8):
        b, half = c // 2, c % 2
        m = dict(shared)
        m.update(consts)
        m["x_main"] = np.ascontiguousarray(x[b, 2048 * half:2048 * (half + 1), :])
        m["x_pre"] = np.ascontiguousarray(x[b, 0:2048, :])
        m["flag"] = np.full((128, 1), float(half), dtype=np.float32)
        in_maps.append(m)
    return in_maps


def kernel(L0_attn_hidden_states, L0_attn_Wq, L0_attn_Wk, L0_attn_Wv, L0_attn_Wo,
           L0_attn_Wfq, L0_attn_Wfk, L0_ffn_gate_weight, L0_ffn_gate_up_proj,
           L0_ffn_down_proj, L0_ffn_similarity):
    x = np.ascontiguousarray(np.asarray(L0_attn_hidden_states, dtype=np.float32))
    shared = {
        "wq": np.ascontiguousarray(L0_attn_Wq, dtype=np.float32),
        "wk": np.ascontiguousarray(L0_attn_Wk, dtype=np.float32),
        "wv": np.ascontiguousarray(L0_attn_Wv, dtype=np.float32),
        "wo": np.ascontiguousarray(L0_attn_Wo, dtype=np.float32),
        "wfq": np.ascontiguousarray(L0_attn_Wfq, dtype=np.float32),
        "wfk": np.ascontiguousarray(L0_attn_Wfk, dtype=np.float32),
        "gate": np.ascontiguousarray(L0_ffn_gate_weight, dtype=np.float32),
        "gup": np.ascontiguousarray(L0_ffn_gate_up_proj, dtype=np.float32),
        "gdn": np.ascontiguousarray(L0_ffn_down_proj, dtype=np.float32),
        "sim": np.ascontiguousarray(L0_ffn_similarity, dtype=np.float32),
    }
    in_maps = make_in_maps(x, shared)
    nc = bass.Bass("TRN2", target_bir_lowering=False)
    build_program(nc)
    res = run_bass_kernel_spmd(nc, in_maps, core_ids=list(range(8)))
    out = np.empty((B, L, D), dtype=np.float32)
    for c in range(8):
        b, half = c // 2, c % 2
        out[b, 2048 * half:2048 * (half + 1), :] = res.results[c]["out"]
    return out


# version 09 (final)
# speedup vs baseline: 1.0891x; 1.0891x over previous
import contextlib
import numpy as np
import ml_dtypes
import concourse.bass as bass
import concourse.mybir as mybir
from concourse.bass_utils import run_bass_kernel_spmd

F32 = mybir.dt.float32
BF16 = mybir.dt.bfloat16
AF = mybir.ActivationFunctionType
ALU = mybir.AluOpType
AX = mybir.AxisListType
BF = ml_dtypes.bfloat16

B, L, D = 4, 4096, 2048
H, HD, FD = 32, 64, 64
TOK = 2048
NG = 2
GT = TOK // NG
NCH = GT // 128
NHG = 4
HPG = H // NHG
NE = 32
DFF = 768
GU2 = 2 * DFF
KP = D // 128
NTP = TOK // 128
EPS_LN = 1e-5
EPS_AT = 1e-12
RD = 1.0 / D

ENGS = ("sync", "scalar", "vector", "tensor", "gpsimd")


class Sem:
    def __init__(self, handle):
        self.h = handle
        self.n = 0

    def bump(self, k):
        self.n += k
        return self.n


class Plan:

    def __init__(self):
        self.prog = {e: [] for e in ENGS}
        self.auto = {}

    def em(self, eng, fn):
        self.prog[eng].append(fn)

    def wait(self, eng, sem, val):
        if val > 0:
            self.em(eng, lambda E, s=sem.h, v=val: E.wait_ge(s, v))

    def dma(self, eng, out, in_, sem, transpose=False, accum=False):
        def f(E, out=out, in_=in_, t=transpose, a=accum, s=sem.h):
            kw = {}
            if t:
                kw["transpose"] = True
            if a:
                kw["accum_op"] = ALU.add
            E.dma_start(out=out, in_=in_, **kw).then_inc(s, 16)
        self.em(eng, f)
        return sem.bump(16)

    def op(self, eng, name, *args, inc=None, **kw):
        auto = self.auto.get(eng)
        if auto is not None:
            self.wait(eng, auto, auto.n)
            inc = auto
        def f(E, name=name, args=args, kw=kw, inc=inc):
            ins = getattr(E, name)(*args, **kw)
            if inc is not None:
                ins.then_inc(inc.h, 1)
        self.em(eng, f)
        return inc.bump(1) if inc is not None else None

    def mm(self, out, lhsT, rhs, start, stop, inc=None):
        def f(T, out=out, l=lhsT, r=rhs, s=start, t=stop, inc=inc):
            ins = T.matmul(out, l, r, start=s, stop=t)
            if inc is not None:
                ins.then_inc(inc.h, 1)
        self.em("tensor", f)
        return inc.bump(1) if inc is not None else None


def build_program(nc, cfg=None):
    cfg = dict(cfg or {})
    do_attn = cfg.get("attn", True)
    do_moe = cfg.get("moe", True)
    n_exp = cfg.get("n_experts", NE)

    x_main = nc.dram_tensor("x_main", [TOK, D], F32, kind="ExternalInput").ap()
    x_pre = nc.dram_tensor("x_pre", [TOK, D], F32, kind="ExternalInput").ap()
    flag = nc.dram_tensor("flag", [128, 1], F32, kind="ExternalInput").ap()
    wq = nc.dram_tensor("wq", [D, D], F32, kind="ExternalInput").ap()
    wk = nc.dram_tensor("wk", [D, D], F32, kind="ExternalInput").ap()
    wv = nc.dram_tensor("wv", [D, D], F32, kind="ExternalInput").ap()
    wo = nc.dram_tensor("wo", [D, D], F32, kind="ExternalInput").ap()
    wfq = nc.dram_tensor("wfq", [H, HD, FD], F32, kind="ExternalInput").ap()
    wfk = nc.dram_tensor("wfk", [H, HD, FD], F32, kind="ExternalInput").ap()
    gate = nc.dram_tensor("gate", [NE, D], F32, kind="ExternalInput").ap()
    gup = nc.dram_tensor("gup", [NE, GU2, D], F32, kind="ExternalInput").ap()
    gdn = nc.dram_tensor("gdn", [NE, D, DFF], F32, kind="ExternalInput").ap()
    sim = nc.dram_tensor("sim", [NE, NE], F32, kind="ExternalInput").ap()
    maskU_f = nc.dram_tensor("maskU_f", [128, 128], F32, kind="ExternalInput").ap()
    maskU_b = nc.dram_tensor("maskU_b", [128, 128], BF16, kind="ExternalInput").ap()
    ident_f = nc.dram_tensor("ident_f", [128, 128], F32, kind="ExternalInput").ap()
    ones_col_b = nc.dram_tensor("ones_col_b", [128, 1], BF16, kind="ExternalInput").ap()
    ones_row_b = nc.dram_tensor("ones_row_b", [1, 128], BF16, kind="ExternalInput").ap()
    zero_b = nc.dram_tensor("zero_b", [128, 128], BF16, kind="ExternalInput").ap()

    out_hbm = nc.dram_tensor("out", [TOK, D], F32, kind="ExternalOutput").ap()

    wv_bf = nc.dram_tensor("wv_bf", [D, D], BF16, kind="Internal").ap()
    wo_bf = nc.dram_tensor("wo_bf", [D, D], BF16, kind="Internal").ap()
    gate_bf = nc.dram_tensor("gate_bf", [NE, D], BF16, kind="Internal").ap()
    gup_bf = nc.dram_tensor("gup_bf", [NE, GU2, D], BF16, kind="Internal").ap()
    gdn_bf = nc.dram_tensor("gdn_bf", [NE, D, DFF], BF16, kind="Internal").ap()
    ln_scr = nc.dram_tensor("ln_scr", [TOK, D], BF16, kind="Internal").ap()
    ln2_scr = nc.dram_tensor("ln2_scr", [TOK, D], BF16, kind="Internal").ap()
    wqf_scr = nc.dram_tensor("wqf_scr", [NHG, 128, KP * 512], BF16, kind="Internal").ap()
    wkf_scr = nc.dram_tensor("wkf_scr", [NHG, 128, KP * 512], BF16, kind="Internal").ap()

    plan = Plan()
    stack = contextlib.ExitStack()
    E = stack.enter_context

    sLD = Sem(E(nc.semaphore("sLD")))
    sWS = Sem(E(nc.semaphore("sWS")))
    sXL = Sem(E(nc.semaphore("sXL")))
    sSC = Sem(E(nc.semaphore("sSC")))
    sGL = Sem(E(nc.semaphore("sGL")))
    sSW = Sem(E(nc.semaphore("sSW")))
    sYA = Sem(E(nc.semaphore("sYA")))
    sPE = Sem(E(nc.semaphore("sPE")))
    sDV = Sem(E(nc.semaphore("sDV")))
    sAC = Sem(E(nc.semaphore("sAC")))
    plan.auto["vector"] = sDV
    plan.auto["scalar"] = sAC

    pbank = [E(nc.psum_tensor(f"pb{i}", [128, 512], F32)) for i in range(8)]
    P_KVA, P_KVB, P_KRA, P_KRB, P_ZK, P_ZQ, P_T1, P_T2 = pbank

    def kv_region(h):
        bank = P_KVA if h < 16 else P_KVB
        hh = h % 16
        return bank[64 * (hh % 2):64 * (hh % 2) + 64,
                    64 * (hh // 2):64 * (hh // 2) + 64]

    def krow_region(h):
        bank = P_KRA if h < 16 else P_KRB
        hh = h % 16
        base = 64 * (hh % 2)
        return bank[base:base + 1, 64 * (hh // 2):64 * (hh // 2) + 64]

    ekT_ps = P_T1[0:64, 0:128]
    st_ps = P_T1[:, 128:256]
    y_ps = P_T1[0:64, 256:384]
    fqsT_ps = P_T1[0:64, 384:512]
    kcum_ps = P_T2[:, 0:512]

    c_maskU_f = E(nc.sbuf_tensor("c_maskU_f", [128, 128], F32))
    c_maskU_b = E(nc.sbuf_tensor("c_maskU_b", [128, 128], BF16))
    c_ident_f = E(nc.sbuf_tensor("c_ident_f", [128, 128], F32))
    c_ones_col = E(nc.sbuf_tensor("c_ones_col", [128, 1], BF16))
    c_ones_row = E(nc.sbuf_tensor("c_ones_row", [1, 128], BF16))
    c_zero = E(nc.sbuf_tensor("c_zero", [128, 128], BF16))
    c_flag = E(nc.sbuf_tensor("c_flag", [128, 1], F32))
    sim_sb = E(nc.sbuf_tensor("sim_sb", [32, 32], BF16))

    ln_stack = contextlib.ExitStack()
    LE = ln_stack.enter_context
    lnx = [LE(nc.sbuf_tensor(f"lnx{i}", [128, D], F32)) for i in range(2)]
    lnsq = LE(nc.sbuf_tensor("lnsq", [128, D // 2], F32))
    lnout = [LE(nc.sbuf_tensor(f"lnout{i}", [128, D], BF16)) for i in range(2)]
    st_nsum = LE(nc.sbuf_tensor("st_nsum", [128, 1], F32))
    st_ssq = LE(nc.sbuf_tensor("st_ssq", [128, 1], F32))
    st_nmean = LE(nc.sbuf_tensor("st_nmean", [128, 1], F32))
    st_var = LE(nc.sbuf_tensor("st_var", [128, 1], F32))
    st_rstd = LE(nc.sbuf_tensor("st_rstd", [128, 1], F32))

    for dst, src in [(c_maskU_f, maskU_f), (c_maskU_b, maskU_b),
                     (c_ident_f, ident_f), (c_ones_col, ones_col_b),
                     (c_ones_row, ones_row_b), (c_zero, zero_b),
                     (c_flag, flag)]:
        plan.dma("sync", dst[:], src, sLD)
    consts_ld = sLD.n
    for eng in ENGS:
        plan.wait(eng, sLD, consts_ld)

    sim_ld = plan.dma("gpsimd", sim_sb[:], sim, sGL)

    lnout_store = [0, 0]
    ln_scr_free_ld = [0]
    state_copied = {h: 0 for h in range(H)}
    kv_pe = {h: 0 for h in range(H)}
    x2_store = [0, 0]

    lnx_dv = [0, 0]

    def ln_phase(src, g, dst):
        for i in range(NCH):
            par = i % 2
            buf = lnx[par]
            lb = lnout[par]
            row = 128 * (NCH * g + i)
            plan.wait("scalar", sDV, lnx_dv[par])
            v = plan.dma("scalar", buf[:], src[row:row + 128, :], sXL)
            plan.wait("vector", sXL, v)
            plan.op("vector", "reduce_sum", st_nsum[:], buf[:], AX.X, negate=True, inc=sDV)
            plan.op("vector", "tensor_mul", lnsq[:], buf[:, 0:D // 2], buf[:, 0:D // 2], inc=sDV)
            plan.op("vector", "reduce_sum", st_ssq[:], lnsq[:], AX.X, inc=sDV)
            plan.op("vector", "tensor_mul", lnsq[:], buf[:, D // 2:D], buf[:, D // 2:D], inc=sDV)
            plan.op("vector", "reduce_sum", st_var[:], lnsq[:], AX.X, inc=sDV)
            plan.op("vector", "tensor_add", st_ssq[:], st_ssq[:], st_var[:], inc=sDV)
            plan.op("vector", "tensor_scalar_mul", st_nmean[:], st_nsum[:], RD, inc=sDV)
            plan.op("vector", "tensor_scalar_mul", st_ssq[:], st_ssq[:], RD, inc=sDV)
            plan.op("vector", "tensor_mul", st_nsum[:], st_nmean[:], st_nmean[:], inc=sDV)
            plan.op("vector", "tensor_sub", st_var[:], st_ssq[:], st_nsum[:], inc=sDV)
            dvv = plan.op("vector", "tensor_scalar", st_var[:], st_var[:], EPS_LN,
                          None, ALU.add, inc=sDV)
            plan.wait("scalar", sDV, dvv)
            ac = plan.op("scalar", "activation", st_rstd[:], st_var[:], AF.Ln, inc=sAC)
            plan.wait("vector", sAC, ac)
            dvv = plan.op("vector", "tensor_scalar", st_rstd[:], st_rstd[:], -0.5,
                          None, ALU.mult, inc=sDV)
            plan.wait("scalar", sDV, dvv)
            ac = plan.op("scalar", "activation", st_rstd[:], st_rstd[:], AF.Exp, inc=sAC)
            plan.wait("vector", sAC, ac)
            plan.wait("vector", sSC, lnout_store[par])
            dvv = plan.op("vector", "tensor_scalar", lb[:], buf[:],
                          st_nmean[:, 0:1], st_rstd[:, 0:1], ALU.add, ALU.mult, inc=sDV)
            lnx_dv[par] = dvv
            plan.wait("scalar", sDV, dvv)
            if i == 0:
                plan.wait("scalar", sLD, ln_scr_free_ld[0])
            lnout_store[par] = plan.dma(
                "scalar", dst[row:row + 128, :], lb[:], sSC)
        return sSW.n


    if do_attn:
        at = contextlib.ExitStack()
        A = at.enter_context
        lnT_buf = A(nc.sbuf_tensor("lnT_buf", [128, KP * GT], BF16, is_transpose_dest=True))
        v_buf = A(nc.sbuf_tensor("v_buf", [128, NCH * D], BF16))
        yt_buf = A(nc.sbuf_tensor("yt_buf", [128, KP * GT], BF16))
        wt_buf = A(nc.sbuf_tensor("wt_buf", [128, KP * 1024], BF16, is_transpose_dest=True))
        wqkf_buf = wt_buf
        wfq_sb = A(nc.sbuf_tensor("wfq_sb", [64, H * FD], BF16))
        wfk_sb = A(nc.sbuf_tensor("wfk_sb", [64, H * FD], BF16))
        wrow_buf = A(nc.sbuf_tensor("wrow", [64, 4 * D], BF16))
        state_sb = A(nc.sbuf_tensor("state_sb", [128, H * HD], BF16))
        krow_sb = A(nc.sbuf_tensor("krow_sb", [1, H * FD], BF16))
        x2h = [A(nc.sbuf_tensor(f"x2h{i}", [128, 1024], F32)) for i in range(2)]
        ek8 = A(nc.sbuf_tensor("ek8", [128, HPG * 64], F32))
        eq8 = A(nc.sbuf_tensor("eq8", [128, HPG * 64], F32))
        fkb8 = A(nc.sbuf_tensor("fkb8", [128, HPG * 64], BF16))
        ekT8 = A(nc.sbuf_tensor("ekT8", [64, HPG * 128], BF16))
        rk8 = A(nc.sbuf_tensor("rk8", [128, HPG], F32))
        fqs_sb = A(nc.sbuf_tensor("fqs_sb", [128, 64], F32))
        fqsT_sb = A(nc.sbuf_tensor("fqsT_sb", [128, 128], BF16))
        sm_sb = A(nc.sbuf_tensor("sm_sb", [128, 128], BF16))
        dtmp8_sb = ek8
        rden8 = A(nc.sbuf_tensor("rden8", [128, HPG], F32))

        plan.op("vector", "memset", state_sb[:], 0.0)
        plan.op("vector", "memset", fqsT_sb[:], 0.0)
        dbg_core = None
        if cfg.get("dbg_core"):
            shapes = {"kcum": (128, 64), "st": (128, 128), "rden": (128, 1),
                      "fqs": (128, 64), "sm": (128, 128), "eq": (128, 64),
                      "ek": (128, 64), "fkb": (128, 64), "ekT": (64, 128),
                      "rk": (128, 1)}
            bufs = {nm: A(nc.sbuf_tensor(f"dbgc_{nm}", list(sh), F32))
                    for nm, sh in shapes.items()}
            dbg_core = {"armed": True, "bufs": bufs, "shapes": shapes}

        plan.dma("gpsimd", wfq_sb[:].rearrange("d (h f) -> d h f", f=FD),
                 wfq.rearrange("h d f -> d h f"), sGL)
        wf_ld = plan.dma("gpsimd", wfk_sb[:].rearrange("d (h f) -> d h f", f=FD),
                         wfk.rearrange("h d f -> d h f"), sGL)
        plan.wait("tensor", sGL, wf_ld)
        wrow_pe = [0, 0]
        for grp in range(NHG):
            plan.wait("vector", sWS, sWS.n)
            for side, (w_in, wf_sb, scr) in enumerate(
                    [(wq, wfq_sb, wqf_scr), (wk, wfk_sb, wkf_scr)]):
                wb = wrow_buf
                for hh in range(HPG):
                    h = grp * HPG + hh
                    if hh % 4 == 0:
                        plan.wait("gpsimd", sPE, wrow_pe[0])
                        ld = plan.dma(
                            "gpsimd",
                            wb[:].rearrange("p (hh d) -> p hh d", d=D),
                            w_in[512 * grp + 256 * (hh // 4):
                                 512 * grp + 256 * (hh // 4) + 256, :].rearrange(
                                "(hh p) d -> p hh d", p=64), sGL)
                        plan.wait("tensor", sGL, ld)
                    pe_half = []
                    for p in range(KP):
                        bank = P_ZK if p < 8 else P_ZQ
                        col = 64 * (p % 8)
                        last = plan.mm(bank[:, col:col + 64],
                                       wb[:, D * (hh % 4) + 128 * p:
                                          D * (hh % 4) + 128 * (p + 1)],
                                       wf_sb[:, 64 * h:64 * h + 64],
                                       start=True, stop=True,
                                       inc=(sPE if (p % 8) == 7 else None))
                        if (p % 8) == 7:
                            pe_half.append(last)
                    wrow_pe[0] = pe_half[-1]
                    reg = wqkf_buf[:, (side * KP * 512):((side + 1) * KP * 512)]
                    reg3 = reg.rearrange("a (p f) -> a p f", f=512)
                    dvl = 0
                    for half, bank in enumerate((P_ZK, P_ZQ)):
                        plan.wait("vector", sPE, pe_half[half])
                        src3 = bank[:, 0:512].rearrange("a (p f) -> a p f", f=64)
                        dvl = plan.op("vector", "tensor_copy",
                                      reg3[:, 8 * half:8 * half + 8,
                                           64 * hh:64 * hh + 64],
                                      src3, inc=sDV)
                    plan.wait("tensor", sDV, dvl)
            plan.wait("sync", sDV, sDV.n)
            plan.dma("sync", wqf_scr[grp], wqkf_buf[:, 0:KP * 512], sWS)
            plan.dma("sync", wkf_scr[grp], wqkf_buf[:, KP * 512:2 * KP * 512], sWS)
        wqkf_stored = sWS.n

        plan.dma("gpsimd", wv_bf, wv, sSW)
        wv_cast = sSW.n
        plan.dma("gpsimd", wo_bf, wo, sSW)
        wo_cast = sSW.n
        plan.dma("gpsimd", gate_bf, gate, sSW)
        gate_cast = sSW.n
        exp_cast = []
        for e in range(n_exp):
            plan.dma("gpsimd", gup_bf[e], gup[e], sSW)
            plan.dma("gpsimd", gdn_bf[e], gdn[e], sSW)
            exp_cast.append(sSW.n)

        for bank in (P_KVA, P_KVB, P_KRA, P_KRB):
            for q in range(4):
                plan.mm(bank[:, 128 * q:128 * (q + 1)], c_zero[:], c_maskU_b[:],
                        start=True, stop=True)
        init_pe = plan.mm(P_T2[:, 256:384], c_zero[:], c_maskU_b[:],
                          start=True, stop=True, inc=sPE)

        def lnT_load(g):
            plan.wait("sync", sSC, lnout_store[0])
            plan.wait("sync", sSC, lnout_store[1])
            plan.wait("sync", sPE, sPE.n)
            for p in range(KP):
                plan.dma("sync", lnT_buf[:, GT * p:GT * (p + 1)],
                         ln_scr[GT * g:GT * (g + 1), 128 * p:128 * (p + 1)],
                         sLD, transpose=True)
            ln_scr_free_ld[0] = sLD.n
            return sLD.n

        def v_proj(wait_lnT):
            for j in range(2):
                plan.wait("sync", sSW, wv_cast)
                plan.wait("sync", sPE, sPE.n)
                for p in range(KP):
                    plan.dma("sync", wt_buf[:, 1024 * p:1024 * (p + 1)],
                             wv_bf[1024 * j:1024 * (j + 1), 128 * p:128 * (p + 1)],
                             sLD, transpose=True)
                wtl = sLD.n
                plan.wait("tensor", sLD, max(wtl, wait_lnT))
                plan.wait("tensor", sDV, sDV.n)
                plan.wait("tensor", sAC, sAC.n)
                for tc in range(NCH):
                    for j2 in range(2):
                        bank = pbank[4 + ((tc * 2 + j2) % 3)]
                        pe_l = None
                        for p in range(KP):
                            pe_l = plan.mm(
                                bank[:, 0:512],
                                lnT_buf[:, GT * p + 128 * tc:GT * p + 128 * (tc + 1)],
                                wt_buf[:, 1024 * p + 512 * j2:1024 * p + 512 * (j2 + 1)],
                                start=(p == 0), stop=(p == KP - 1),
                                inc=(sPE if p == KP - 1 else None))
                        plan.wait("vector", sPE, pe_l)
                        dv = plan.op("vector", "tensor_copy",
                                     v_buf[:, D * tc + 1024 * j + 512 * j2:
                                           D * tc + 1024 * j + 512 * (j2 + 1)],
                                     bank[:, 0:512], inc=sDV)
                        plan.wait("tensor", sDV, dv)

        def head_chunk_kside(hg, c, prefix):
            plan.wait("tensor", sDV, sDV.n)
            plan.wait("tensor", sAC, sAC.n)
            pe = None
            zoff = KP * 512
            for p in range(KP):
                pe = plan.mm(P_ZK[:, 0:512],
                             lnT_buf[:, GT * p + 128 * c:GT * p + 128 * (c + 1)],
                             wqkf_buf[:, zoff + 512 * p:zoff + 512 * (p + 1)],
                             start=(p == 0), stop=(p == KP - 1),
                             inc=(sPE if p == KP - 1 else None))
            plan.wait("scalar", sPE, pe)
            ac = plan.op("scalar", "activation", ek8[:], P_ZK[:, 0:512], AF.Exp,
                         inc=sAC)
            plan.wait("vector", sAC, ac)
            plan.op("vector", "reduce_sum",
                    rk8[:].rearrange("a (h f) -> a h f", f=1),
                    ek8[:].rearrange("a (h f) -> a h f", f=64), AX.X, inc=sDV)
            dv = plan.op("vector", "reciprocal", rk8[:], rk8[:], inc=sDV)
            if prefix:
                dv = plan.op("vector", "tensor_scalar", rk8[:], rk8[:],
                             c_flag[:, 0:1], None, ALU.mult, inc=sDV)
            plan.wait("scalar", sDV, dv)
            for hh in range(HPG):
                h = hg * HPG + hh
                ac = plan.op("scalar", "activation", fkb8[:, 64 * hh:64 * hh + 64],
                             ek8[:, 64 * hh:64 * hh + 64], AF.Copy,
                             scale=rk8[:, hh:hh + 1], inc=sAC)
                plan.wait("tensor", sAC, ac)
                vsl = v_buf[:, D * c + 64 * h:D * c + 64 * h + 64]
                plan.mm(kv_region(h), fkb8[:, 64 * hh:64 * hh + 64], vsl,
                        start=False, stop=True)
                kv_pe[h] = plan.mm(krow_region(h), c_ones_col[:],
                                   fkb8[:, 64 * hh:64 * hh + 64],
                                   start=False, stop=True, inc=sPE)
                if not prefix:
                    pe3 = plan.op("tensor", "transpose", ekT_ps,
                                  ek8[:, 64 * hh:64 * hh + 64], c_ident_f[:], inc=sPE)
                    plan.wait("vector", sPE, pe3)
                    dv = plan.op("vector", "tensor_copy",
                                 ekT8[:, 128 * hh:128 * (hh + 1)], ekT_ps, inc=sDV)
                    plan.wait("tensor", sDV, dv)
            return sPE.n

        def head_chunk_qside(hg, c):
            plan.wait("tensor", sDV, sDV.n)
            plan.wait("tensor", sAC, sAC.n)
            pe = None
            for p in range(KP):
                pe = plan.mm(P_ZQ[:, 0:512],
                             lnT_buf[:, GT * p + 128 * c:GT * p + 128 * (c + 1)],
                             wqkf_buf[:, 512 * p:512 * (p + 1)],
                             start=(p == 0), stop=(p == KP - 1),
                             inc=(sPE if p == KP - 1 else None))
            plan.wait("scalar", sPE, pe)
            ac_eq = plan.op("scalar", "activation", eq8[:], P_ZQ[:, 0:512], AF.Exp,
                            inc=sAC)
            for h2 in range(hg * HPG, (hg + 1) * HPG):
                plan.wait("tensor", sDV, state_copied[h2])
            plan.mm(kcum_ps, c_maskU_b[:], fkb8[:], start=True, stop=False)
            pe1 = plan.mm(kcum_ps, c_ones_row[:],
                          krow_sb[0:1, 512 * hg:512 * (hg + 1)],
                          start=False, stop=True, inc=sPE)
            plan.wait("vector", sPE, pe1)
            plan.wait("vector", sAC, ac_eq)
            plan.op("vector", "scalar_tensor_tensor", dtmp8_sb[:], kcum_ps,
                    EPS_AT, eq8[:], ALU.add, ALU.mult, inc=sDV)
            plan.op("vector", "reduce_sum",
                    rden8[:].rearrange("a (h f) -> a h f", f=1),
                    dtmp8_sb[:].rearrange("a (h f) -> a h f", f=64), AX.X, inc=sDV)
            dv = plan.op("vector", "reciprocal", rden8[:], rden8[:], inc=sDV)
            plan.wait("scalar", sDV, dv)
            for hh in range(HPG):
                h = hg * HPG + hh
                ac = plan.op("scalar", "activation", fqs_sb[:],
                             eq8[:, 64 * hh:64 * hh + 64], AF.Copy,
                             scale=rden8[:, hh:hh + 1], inc=sAC)
                plan.wait("tensor", sAC, ac)
                pe2 = plan.op("tensor", "transpose", fqsT_ps, fqs_sb[:],
                              c_ident_f[:], inc=sPE)
                plan.wait("vector", sPE, pe2)
                dv = plan.op("vector", "tensor_copy", fqsT_sb[0:64, :], fqsT_ps, inc=sDV)
                plan.wait("tensor", sDV, dv)
                pe3 = plan.mm(st_ps, ekT8[:, 128 * hh:128 * (hh + 1)],
                              fqsT_sb[0:64, :], start=True, stop=True, inc=sPE)
                plan.wait("vector", sPE, pe3)
                dv = plan.op("vector", "scalar_tensor_tensor", sm_sb[:], st_ps,
                             rk8[:, hh:hh + 1], c_maskU_f[:], ALU.mult, ALU.mult,
                             inc=sDV)
                plan.wait("tensor", sDV, dv)
                vsl = v_buf[:, D * c + 64 * h:D * c + 64 * h + 64]
                plan.mm(y_ps, vsl, sm_sb[:], start=True, stop=False)
                pe4 = plan.mm(y_ps, state_sb[:, 64 * h:64 * h + 64], fqsT_sb[:],
                              start=False, stop=True, inc=sPE)
                plan.wait("scalar", sPE, pe4)
                ac = plan.op("scalar", "copy",
                             yt_buf[64 * (h % 2):64 * (h % 2) + 64,
                                    GT * (h // 2) + 128 * c:
                                    GT * (h // 2) + 128 * (c + 1)],
                             y_ps, inc=sAC)
                plan.wait("tensor", sAC, ac)
                plan.wait("vector", sPE, kv_pe[h])
                plan.op("vector", "tensor_copy", state_sb[0:64, 64 * h:64 * h + 64],
                        kv_region(h), inc=sDV)
                dv = plan.op("vector", "tensor_copy",
                             krow_sb[0:1, 64 * h:64 * h + 64], krow_region(h),
                             inc=sDV)
                state_copied[h] = dv
                if dbg_core is not None and hg == 0 and hh == 0 and dbg_core.get("armed"):
                    dbg_core["armed"] = False
                    bufs = dbg_core["bufs"]
                    plan.op("vector", "tensor_copy", bufs["kcum"][:], dtmp8_sb[:, 0:64], inc=sDV)
                    plan.op("vector", "tensor_copy", bufs["st"][:], sm_sb[:], inc=sDV)
                    plan.op("vector", "tensor_copy", bufs["rden"][:], rden8[:, 0:1], inc=sDV)
                    plan.op("vector", "tensor_copy", bufs["fqs"][:], fqs_sb[:], inc=sDV)
                    plan.op("vector", "tensor_copy", bufs["sm"][:], sm_sb[:], inc=sDV)
                    plan.op("vector", "tensor_copy", bufs["eq"][:], eq8[:, 0:64], inc=sDV)
                    plan.op("vector", "tensor_copy", bufs["ek"][:], ek8[:, 0:64], inc=sDV)
                    plan.op("vector", "tensor_copy", bufs["fkb"][:], fkb8[:, 0:64], inc=sDV)
                    plan.op("vector", "tensor_copy", bufs["ekT"][:], ekT8[:, 0:128], inc=sDV)
                    dv2 = plan.op("vector", "tensor_copy", bufs["rk"][:], rk8[:, 0:1], inc=sDV)
                    plan.wait("tensor", sDV, dv2)

        for g in range(NG):
            ln_phase(x_pre, g, ln_scr)
            lnT_ld = lnT_load(g)
            v_proj(lnT_ld)
            for hg in range(NHG):
                plan.wait("sync", sWS, wqkf_stored)
                plan.wait("sync", sPE, sPE.n)
                plan.dma("sync", wqkf_buf[:, KP * 512:2 * KP * 512], wkf_scr[hg], sLD)
                plan.wait("tensor", sLD, sLD.n)
                if g == 0 and hg == 0:
                    plan.wait("tensor", sPE, init_pe)
                for c in range(NCH):
                    head_chunk_kside(hg, c, prefix=True)
        plan.wait("vector", sPE, sPE.n)
        for h in range(H):
            plan.op("vector", "tensor_copy", state_sb[0:64, 64 * h:64 * h + 64],
                    kv_region(h), inc=sDV)
            dv = plan.op("vector", "tensor_copy",
                         krow_sb[0:1, 64 * h:64 * h + 64], krow_region(h), inc=sDV)
            state_copied[h] = dv
        if cfg.get("dbg_attn"):
            dt_ = nc.dram_tensor("dbg_state0", [64, H * HD], BF16,
                                 kind="ExternalOutput").ap()
            plan.wait("gpsimd", sDV, sDV.n)
            plan.dma("gpsimd", dt_, state_sb[0:64, :], sSW)
            dt_ = nc.dram_tensor("dbg_krow0", [1, H * FD], BF16,
                                 kind="ExternalOutput").ap()
            plan.dma("gpsimd", dt_, krow_sb[:], sSW)

        for g in range(NG):
            ln_phase(x_main, g, ln_scr)
            lnT_ld = lnT_load(g)
            v_proj(lnT_ld)
            for hg in range(NHG):
                plan.wait("sync", sPE, sPE.n)
                plan.dma("sync", wqkf_buf[:, 0:KP * 512], wqf_scr[hg], sLD)
                plan.dma("sync", wqkf_buf[:, KP * 512:2 * KP * 512], wkf_scr[hg], sLD)
                plan.wait("tensor", sLD, sLD.n)
                for c in range(NCH):
                    head_chunk_kside(hg, c, prefix=False)
                    head_chunk_qside(hg, c)
            if cfg.get("dbg_attn") and g == 0:
                dt_ = nc.dram_tensor("dbg_yt0", [128, KP * GT], BF16,
                                     kind="ExternalOutput").ap()
                plan.wait("gpsimd", sAC, sAC.n)
                plan.wait("gpsimd", sDV, sDV.n)
                plan.dma("gpsimd", dt_, yt_buf[:], sSW)
            for j in range(2):
                plan.wait("sync", sSW, wo_cast)
                plan.wait("sync", sPE, sPE.n)
                for p in range(KP):
                    plan.dma("sync", wt_buf[:, 1024 * p:1024 * (p + 1)],
                             wo_bf[1024 * j:1024 * (j + 1), 128 * p:128 * (p + 1)],
                             sLD, transpose=True)
                wtl = sLD.n
                plan.wait("tensor", sLD, wtl)
                plan.wait("tensor", sDV, sDV.n)
                plan.wait("tensor", sAC, sAC.n)
                plan.wait("scalar", sDV, sDV.n)
                for tc in range(NCH):
                    par = tc % 2
                    row = 128 * (NCH * g + tc)
                    plan.wait("scalar", sDV, lnx_dv[par])
                    v = plan.dma("scalar", lnx[par][:, 0:1024],
                                 x_main[row:row + 128, 1024 * j:1024 * (j + 1)], sXL)
                    for dt2 in range(2):
                        bank = pbank[4 + ((tc * 2 + dt2) % 2)]
                        pe_l = None
                        for p in range(KP):
                            pe_l = plan.mm(
                                bank[:, 0:512],
                                yt_buf[:, GT * p + 128 * tc:GT * p + 128 * (tc + 1)],
                                wt_buf[:, 1024 * p + 512 * dt2:
                                       1024 * p + 512 * (dt2 + 1)],
                                start=(p == 0), stop=(p == KP - 1),
                                inc=(sPE if p == KP - 1 else None))
                        plan.wait("vector", sPE, pe_l)
                        if dt2 == 0:
                            plan.wait("vector", sXL, v)
                            plan.wait("vector", sSC, x2_store[par])
                        dv = plan.op("vector", "tensor_add",
                                     x2h[par][:, 512 * dt2:512 * (dt2 + 1)],
                                     bank[:, 0:512],
                                     lnx[par][:, 512 * dt2:512 * (dt2 + 1)], inc=sDV)
                        lnx_dv[par] = dv
                        plan.wait("tensor", sDV, dv)
                    plan.wait("scalar", sDV, sDV.n)
                    x2_store[par] = plan.dma(
                        "scalar", out_hbm[row:row + 128, 1024 * j:1024 * (j + 1)],
                        x2h[par][:], sSC)
        x2_all_stored = ("SC", sSC.n)
        if cfg.get("dbg_wscr"):
            for nm, t in [("dbg_wqf0", wqf_scr), ("dbg_wkf0", wkf_scr)]:
                dt_ = nc.dram_tensor(nm, [128, KP * 512], BF16,
                                     kind="ExternalOutput").ap()
                plan.dma("gpsimd", dt_, t[0], sSW)
            dt_ = nc.dram_tensor("dbg_wqf1", [128, KP * 512], BF16,
                                 kind="ExternalOutput").ap()
            plan.dma("gpsimd", dt_, wqf_scr[1], sSW)
            dt_ = nc.dram_tensor("dbg_wfq", [64, H * FD], BF16,
                                 kind="ExternalOutput").ap()
            plan.dma("gpsimd", dt_, wfq_sb[:], sSW)
        if dbg_core is not None:
            for nm, sh in dbg_core["shapes"].items():
                dt_ = nc.dram_tensor(f"dbgo_{nm}", list(sh), F32,
                                     kind="ExternalOutput").ap()
                plan.wait("gpsimd", sDV, sDV.n)
                plan.dma("gpsimd", dt_, dbg_core["bufs"][nm][:], sSW)
        if cfg.get("dbg_attn"):
            for nm, t, w in [("dbg_yt", yt_buf, KP * GT), ("dbg_v", v_buf, NCH * D),
                             ("dbg_wqkf", wqkf_buf, 2 * KP * 512)]:
                dt_ = nc.dram_tensor(nm, [128, w], BF16, kind="ExternalOutput").ap()
                plan.wait("gpsimd", sPE, sPE.n)
                plan.wait("gpsimd", sAC, sAC.n)
                plan.dma("gpsimd", dt_, t[:], sSW)
            for nm, t, w in [("dbg_state", state_sb, H * HD)]:
                dt_ = nc.dram_tensor(nm, [64, w], BF16, kind="ExternalOutput").ap()
                plan.wait("gpsimd", sDV, sDV.n)
                plan.dma("gpsimd", dt_, t[0:64, :], sSW)
            dt_ = nc.dram_tensor("dbg_krow", [1, H * FD], BF16, kind="ExternalOutput").ap()
            plan.dma("gpsimd", dt_, krow_sb[:], sSW)
        plan.wait("scalar", sSC, x2_all_stored[1])
        for g in range(NG):
            ln_phase(out_hbm, g, ln2_scr)
        ln2_stored = ("SC", sSC.n)
        at.close()
    else:
        plan.dma("gpsimd", out_hbm, x_main, sSW)
        x2_all_stored = ("SW", sSW.n)
        plan.dma("gpsimd", gate_bf, gate, sSW)
        gate_cast = sSW.n
        exp_cast = []
        for e in range(n_exp):
            plan.dma("gpsimd", gup_bf[e], gup[e], sSW)
            plan.dma("gpsimd", gdn_bf[e], gdn[e], sSW)
            exp_cast.append(sSW.n)
        plan.wait("scalar", sSW, x2_all_stored[1])
        for g in range(NG):
            ln_phase(x_main, g, ln2_scr)
        ln2_stored = ("SC", sSC.n)

    ln_stack.close()

    if do_moe:
        mo = contextlib.ExitStack()
        M = mo.enter_context
        ln2T_buf = M(nc.sbuf_tensor("ln2T_buf", [128, KP * TOK], BF16,
                                    is_transpose_dest=True))
        wgu_buf = M(nc.sbuf_tensor("wgu_buf", [128, KP * GU2], BF16,
                                   is_transpose_dest=True))
        wd_buf = M(nc.sbuf_tensor("wd_buf", [128, 6 * D], BF16,
                                  is_transpose_dest=True))
        h_buf = M(nc.sbuf_tensor("h_buf", [128, 6 * TOK], BF16))
        gateT_sb = M(nc.sbuf_tensor("gateT_sb", [128, KP * 32], BF16,
                                    is_transpose_dest=True))
        gs_st = [M(nc.sbuf_tensor(f"gs_st{i}", [128, 512], BF16)) for i in range(2)]
        us_st = [M(nc.sbuf_tensor(f"us_st{i}", [128, 512], BF16)) for i in range(2)]
        y_st = [M(nc.sbuf_tensor(f"y_st{i}", [128, D], F32)) for i in range(2)]
        cmb_sb = M(nc.sbuf_tensor("cmb_sb", [128, NTP * 32], F32))
        prob_sb = M(nc.sbuf_tensor("prob_sb", [128, 32], F32))
        mk_sb = M(nc.sbuf_tensor("mk_sb", [128, 8 * 32], F32))
        mkT_sb = M(nc.sbuf_tensor("mkT_sb", [32, 128], BF16))
        rw_sb = M(nc.sbuf_tensor("rw_sb", [128, 8], F32))
        rwn_sb = M(nc.sbuf_tensor("rwn_sb", [128, 8], F32))
        rtmp_sb = M(nc.sbuf_tensor("rtmp_sb", [128, 32], F32))
        r1_sb = M(nc.sbuf_tensor("r1_sb", [128, 1], F32))
        cj_sb = M(nc.sbuf_tensor("cj_sb", [128, 4], F32))
        best_sb = M(nc.sbuf_tensor("best_sb", [128, 1], F32))
        gj_sb = M(nc.sbuf_tensor("gj_sb", [128, 1], F32))
        eqx_sb = M(nc.sbuf_tensor("eqx_sb", [128, 4], F32))
        coefP_sb = M(nc.sbuf_tensor("coefP_sb", [128, 4], F32))
        coefS_sb = M(nc.sbuf_tensor("coefS_sb", [128, 4], F32))

        ln2_sem = sSC if ln2_stored[0] == "SC" else sSW
        for eng in ENGS:
            plan.wait(eng, sDV, sDV.n)
            plan.wait(eng, sAC, sAC.n)
            plan.wait(eng, sPE, sPE.n)
            plan.wait(eng, ln2_sem, ln2_stored[1])

        for p in range(KP):
            plan.dma("sync", ln2T_buf[:, TOK * p:TOK * (p + 1)],
                     ln2_scr[:, 128 * p:128 * (p + 1)], sLD, transpose=True)
        ln2T_ld = sLD.n
        plan.wait("sync", sSW, gate_cast)
        for p in range(KP):
            plan.dma("sync", gateT_sb[:, 32 * p:32 * (p + 1)],
                     gate_bf[:, 128 * p:128 * (p + 1)], sLD, transpose=True)
        gateT_ld = sLD.n

        plan.wait("tensor", sLD, gateT_ld)
        plan.wait("tensor", sGL, sim_ld)
        for tp in range(NTP):
            bank = pbank[6 + (tp % 2)]
            pe = None
            for p in range(KP):
                pe = plan.mm(bank[:, 0:32],
                             ln2T_buf[:, TOK * p + 128 * tp:TOK * p + 128 * (tp + 1)],
                             gateT_sb[:, 32 * p:32 * (p + 1)],
                             start=(p == 0), stop=(p == KP - 1),
                             inc=(sPE if p == KP - 1 else None))
            plan.wait("vector", sPE, pe)
            plan.wait("scalar", sPE, pe)
            plan.op("vector", "reduce_max", r1_sb[:], bank[:, 0:32], AX.X,
                    negate=True, inc=sDV)
            plan.wait("scalar", sDV, sDV.n)
            ac = plan.op("scalar", "activation", prob_sb[:], bank[:, 0:32], AF.Exp,
                         bias=r1_sb[:, 0:1], inc=sAC)
            plan.wait("vector", sAC, ac)
            for k in range(8):
                plan.op("vector", "reduce_max", rw_sb[:, k:k + 1], prob_sb[:],
                        AX.X, inc=sDV)
                plan.op("vector", "tensor_scalar", mk_sb[:, 32 * k:32 * (k + 1)],
                        prob_sb[:], rw_sb[:, k:k + 1], None, ALU.is_equal, inc=sDV)
                plan.op("vector", "scalar_tensor_tensor", prob_sb[:],
                        mk_sb[:, 32 * k:32 * (k + 1)], -2.0, prob_sb[:],
                        ALU.mult, ALU.add, inc=sDV)
            plan.op("vector", "reduce_sum", r1_sb[:], rw_sb[:], AX.X, inc=sDV)
            plan.op("vector", "reciprocal", r1_sb[:], r1_sb[:], inc=sDV)
            plan.op("vector", "tensor_scalar", rwn_sb[:], rw_sb[:], r1_sb[:, 0:1],
                    None, ALU.mult, inc=sDV)
            plan.op("vector", "tensor_copy", coefP_sb[:], rwn_sb[:, 0:4], inc=sDV)
            for jj in range(4):
                k = 4 + jj
                plan.wait("tensor", sDV, sDV.n)
                plan.op("tensor", "transpose", bank[0:32, 64:192],
                        mk_sb[:, 32 * k:32 * (k + 1)], c_ident_f[:], inc=sPE)
                plan.wait("vector", sPE, sPE.n)
                dv = plan.op("vector", "tensor_copy", mkT_sb[:],
                             bank[0:32, 64:192], inc=sDV)
                plan.wait("tensor", sDV, dv)
                pe = plan.mm(bank[:, 192:224], mkT_sb[:], sim_sb[:],
                             start=True, stop=True, inc=sPE)
                plan.wait("vector", sPE, pe)
                for i in range(4):
                    plan.op("vector", "tensor_mul", rtmp_sb[:], bank[:, 192:224],
                            mk_sb[:, 32 * i:32 * (i + 1)], inc=sDV)
                    plan.op("vector", "reduce_sum", cj_sb[:, i:i + 1], rtmp_sb[:],
                            AX.X, inc=sDV)
                dv = plan.op("vector", "reduce_max", best_sb[:], cj_sb[:], AX.X,
                             inc=sDV)
                plan.wait("tensor", sDV, dv)
                plan.op("vector", "tensor_scalar", gj_sb[:], best_sb[:], 0.5, None,
                        ALU.is_ge, inc=sDV)
                plan.op("vector", "tensor_scalar", eqx_sb[:], cj_sb[:],
                        best_sb[:, 0:1], None, ALU.is_equal, inc=sDV)
                for i in range(1, 4):
                    for i2 in range(i):
                        plan.op("vector", "scalar_tensor_tensor", rtmp_sb[:, 0:1],
                                eqx_sb[:, i2:i2 + 1], -1.0, eqx_sb[:, i:i + 1],
                                ALU.mult, ALU.mult, inc=sDV)
                        plan.op("vector", "tensor_add", eqx_sb[:, i:i + 1],
                                eqx_sb[:, i:i + 1], rtmp_sb[:, 0:1], inc=sDV)
                plan.op("vector", "tensor_mul", rtmp_sb[:, 0:1], gj_sb[:],
                        rwn_sb[:, k:k + 1], inc=sDV)
                for i in range(4):
                    plan.op("vector", "tensor_mul", rtmp_sb[:, 1:2],
                            rtmp_sb[:, 0:1], eqx_sb[:, i:i + 1], inc=sDV)
                    plan.op("vector", "tensor_add", coefP_sb[:, i:i + 1],
                            coefP_sb[:, i:i + 1], rtmp_sb[:, 1:2], inc=sDV)
                plan.op("vector", "tensor_sub", coefS_sb[:, jj:jj + 1],
                        rwn_sb[:, k:k + 1], rtmp_sb[:, 0:1], inc=sDV)
            cmb = cmb_sb[:, 32 * tp:32 * (tp + 1)]
            plan.op("vector", "tensor_scalar", cmb, mk_sb[:, 0:32],
                    coefP_sb[:, 0:1], None, ALU.mult, inc=sDV)
            for i in range(1, 4):
                plan.op("vector", "scalar_tensor_tensor", cmb,
                        mk_sb[:, 32 * i:32 * (i + 1)], coefP_sb[:, i:i + 1], cmb,
                        ALU.mult, ALU.add, inc=sDV)
            for jj in range(4):
                k = 4 + jj
                plan.op("vector", "scalar_tensor_tensor", cmb,
                        mk_sb[:, 32 * k:32 * (k + 1)], coefS_sb[:, jj:jj + 1], cmb,
                        ALU.mult, ALU.add, inc=sDV)
        router_dv = sDV.n
        if cfg.get("dbg_router"):
            for nm, t, w in [("dbg_rw", rw_sb, 8), ("dbg_rwn", rwn_sb, 8),
                             ("dbg_mk", mk_sb, 256), ("dbg_prob", prob_sb, 32),
                             ("dbg_cj", cj_sb, 4), ("dbg_coefP", coefP_sb, 4),
                             ("dbg_coefS", coefS_sb, 4)]:
                dt = nc.dram_tensor(nm, [128, w], F32, kind="ExternalOutput").ap()
                plan.wait("gpsimd", sDV, router_dv)
                plan.dma("gpsimd", dt, t[:], sSW)
        if cfg.get("dbg_cmb"):
            dbg_cmb = nc.dram_tensor("dbg_cmb", [128, NTP * 32], F32,
                                     kind="ExternalOutput").ap()
            plan.wait("gpsimd", sDV, router_dv)
            plan.dma("gpsimd", dbg_cmb, cmb_sb[:], sSW)

        plan.wait("tensor", sLD, ln2T_ld)
        gu_free_pe = 0
        wd_free_pe = 0
        h_free_pe = 0
        x2sem = sSC if x2_all_stored[0] == "SC" else sSW
        plan.wait("gpsimd", x2sem, x2_all_stored[1])
        yacc_prev = 0
        gs_dv = [0, 0]
        y_sw = [0, 0]
        bank_free = {i: (None, 0) for i in range(8)}
        for e in range(n_exp):
            plan.wait("sync", sSW, exp_cast[e])
            plan.wait("sync", sPE, gu_free_pe)
            for p in range(KP):
                plan.dma("sync", wgu_buf[:, GU2 * p:GU2 * (p + 1)],
                         gup_bf[e][:, 128 * p:128 * (p + 1)], sLD, transpose=True)
            wgu_ld = sLD.n
            plan.wait("sync", sPE, wd_free_pe)
            for kf in range(6):
                plan.dma("sync", wd_buf[:, D * kf:D * (kf + 1)],
                         gdn_bf[e][:, 128 * kf:128 * (kf + 1)], sLD, transpose=True)
            wd_ld = sLD.n

            plan.wait("tensor", sLD, wgu_ld)
            plan.wait("tensor", sPE, h_free_pe)
            plan.wait("vector", sPE, h_free_pe)
            it = 0
            for fp in range(6):
                for tt in range(4):
                    bA = (2 * it) % 4
                    bB = bA + 1
                    it += 1
                    for bi in (bA, bB):
                        sem, val = bank_free[bi]
                        if sem is not None:
                            plan.wait("tensor", sem, val)
                    peA = peB = None
                    for p in range(KP):
                        peA = plan.mm(
                            pbank[bA][:, 0:512],
                            wgu_buf[:, GU2 * p + 128 * fp:GU2 * p + 128 * (fp + 1)],
                            ln2T_buf[:, TOK * p + 512 * tt:TOK * p + 512 * (tt + 1)],
                            start=(p == 0), stop=(p == KP - 1),
                            inc=(sPE if p == KP - 1 else None))
                    for p in range(KP):
                        peB = plan.mm(
                            pbank[bB][:, 0:512],
                            wgu_buf[:, GU2 * p + 128 * (fp + 6):GU2 * p + 128 * (fp + 7)],
                            ln2T_buf[:, TOK * p + 512 * tt:TOK * p + 512 * (tt + 1)],
                            start=(p == 0), stop=(p == KP - 1),
                            inc=(sPE if p == KP - 1 else None))
                    par = (fp * 4 + tt) % 2
                    plan.wait("scalar", sPE, peA)
                    plan.wait("scalar", sDV, gs_dv[par])
                    acg = plan.op("scalar", "activation", gs_st[par][:],
                                  pbank[bA][:, 0:512], AF.Silu, inc=sAC)
                    plan.wait("vector", sPE, peB)
                    plan.op("vector", "tensor_copy", us_st[par][:],
                            pbank[bB][:, 0:512], inc=sDV)
                    plan.wait("vector", sAC, acg)
                    dvh = plan.op("vector", "tensor_mul",
                                  h_buf[:, TOK * fp + 512 * tt:TOK * fp + 512 * (tt + 1)],
                                  gs_st[par][:], us_st[par][:], inc=sDV)
                    gs_dv[par] = dvh
                    bank_free[bA] = (sAC, sAC.n)
                    bank_free[bB] = (sDV, dvh)
            gu_free_pe = sPE.n
            h_ready_dv = sDV.n

            plan.wait("tensor", sLD, wd_ld)
            plan.wait("tensor", sDV, h_ready_dv)
            for tp in range(NTP):
                par = tp % 2
                for dt in range(4):
                    bi = 4 + (dt % 2)
                    sem, val = bank_free[bi]
                    if sem is not None:
                        plan.wait("tensor", sem, val)
                    pe_l = None
                    for kf in range(6):
                        pe_l = plan.mm(
                            pbank[bi][:, 0:512],
                            h_buf[:, TOK * kf + 128 * tp:TOK * kf + 128 * (tp + 1)],
                            wd_buf[:, D * kf + 512 * dt:D * kf + 512 * (dt + 1)],
                            start=(kf == 0), stop=(kf == 5),
                            inc=(sPE if kf == 5 else None))
                    plan.wait("scalar", sPE, pe_l)
                    if dt == 0:
                        plan.wait("scalar", sDV, router_dv)
                        plan.wait("scalar", sYA, y_sw[par])
                    ac = plan.op("scalar", "activation",
                                 y_st[par][:, 512 * dt:512 * (dt + 1)],
                                 pbank[bi][:, 0:512], AF.Copy,
                                 scale=cmb_sb[:, 32 * tp + e:32 * tp + e + 1],
                                 inc=sAC)
                    bank_free[bi] = (sAC, ac)
                plan.wait("gpsimd", sAC, sAC.n)
                plan.wait("gpsimd", sYA, yacc_prev)
                y_sw[par] = plan.dma("gpsimd", out_hbm[128 * tp:128 * (tp + 1), :],
                                     y_st[par][:], sYA, accum=True)
            yacc_prev = sYA.n
            wd_free_pe = sPE.n
            h_free_pe = sPE.n
        plan.wait("gpsimd", sYA, sYA.n)
        plan.wait("gpsimd", sSW, sSW.n)
        mo.close()
    else:
        plan.wait("gpsimd", sSW, sSW.n)

    with nc.Block() as block:
        @block.sync
        def _(e):
            for f in plan.prog["sync"]:
                f(e)

        @block.scalar
        def _(e):
            for f in plan.prog["scalar"]:
                f(e)

        @block.vector
        def _(e):
            for f in plan.prog["vector"]:
                f(e)

        @block.tensor
        def _(e):
            for f in plan.prog["tensor"]:
                f(e)

        @block.gpsimd
        def _(e):
            for f in plan.prog["gpsimd"]:
                f(e)

    stack.close()
    return nc


def make_consts():
    maskU = np.triu(np.ones((128, 128), dtype=np.float32))
    return {
        "maskU_f": maskU,
        "maskU_b": maskU.astype(BF),
        "ident_f": np.eye(128, dtype=np.float32),
        "ones_col_b": np.ones((128, 1), dtype=BF),
        "ones_row_b": np.ones((1, 128), dtype=BF),
        "zero_b": np.zeros((128, 128), dtype=BF),
    }


def make_in_maps(x, shared):
    consts = make_consts()
    in_maps = []
    for c in range(8):
        b, half = c // 2, c % 2
        m = dict(shared)
        m.update(consts)
        m["x_main"] = np.ascontiguousarray(x[b, 2048 * half:2048 * (half + 1), :])
        m["x_pre"] = np.ascontiguousarray(x[b, 0:2048, :])
        m["flag"] = np.full((128, 1), float(half), dtype=np.float32)
        in_maps.append(m)
    return in_maps


def kernel(L0_attn_hidden_states, L0_attn_Wq, L0_attn_Wk, L0_attn_Wv, L0_attn_Wo,
           L0_attn_Wfq, L0_attn_Wfk, L0_ffn_gate_weight, L0_ffn_gate_up_proj,
           L0_ffn_down_proj, L0_ffn_similarity):
    x = np.ascontiguousarray(np.asarray(L0_attn_hidden_states, dtype=np.float32))
    shared = {
        "wq": np.ascontiguousarray(L0_attn_Wq, dtype=np.float32),
        "wk": np.ascontiguousarray(L0_attn_Wk, dtype=np.float32),
        "wv": np.ascontiguousarray(L0_attn_Wv, dtype=np.float32),
        "wo": np.ascontiguousarray(L0_attn_Wo, dtype=np.float32),
        "wfq": np.ascontiguousarray(L0_attn_Wfq, dtype=np.float32),
        "wfk": np.ascontiguousarray(L0_attn_Wfk, dtype=np.float32),
        "gate": np.ascontiguousarray(L0_ffn_gate_weight, dtype=np.float32),
        "gup": np.ascontiguousarray(L0_ffn_gate_up_proj, dtype=np.float32),
        "gdn": np.ascontiguousarray(L0_ffn_down_proj, dtype=np.float32),
        "sim": np.ascontiguousarray(L0_ffn_similarity, dtype=np.float32),
    }
    in_maps = make_in_maps(x, shared)
    nc = bass.Bass("TRN2", target_bir_lowering=False)
    build_program(nc)
    res = run_bass_kernel_spmd(nc, in_maps, core_ids=list(range(8)))
    out = np.empty((B, L, D), dtype=np.float32)
    for c in range(8):
        b, half = c // 2, c % 2
        out[b, 2048 * half:2048 * (half + 1), :] = res.results[c]["out"]
    return out
